# version 11
# baseline (speedup 1.0000x reference)
import math
from contextlib import ExitStack

import numpy as np
import ml_dtypes

import concourse.bass as bass
import concourse.mybir as mybir
from concourse.bass_utils import run_bass_kernel_spmd

F32 = mybir.dt.float32
BF16 = mybir.dt.bfloat16
AF = mybir.ActivationFunctionType
ALU = mybir.AluOpType
AX = mybir.AxisListType.X

NC = 8
T = 8192
E = 2048
NCH = 16
TLOC = 1024
NT_LOC = 8
LAMBDA_INIT = 0.8 - 0.6 * math.exp(-0.3 * 12)
EPS = 1e-5
SCALE = 128.0 ** -0.5

KOFF = 0
KSZ = NCH * 128 * TLOC
VOFF = KSZ
VROW = 8 * 258
TILE_V = 128 * VROW
VSZ = NT_LOC * TILE_V
SOFF = VOFF + VSZ
RSZ = SOFF + 16


def band_r0(grp, j):
    return min(3, max(0, j // 8 - 4 * grp))


def slot_last_j(grp, r):
    return 8 * (4 * grp + r) + 7


def build_program(detect_races=False):
    nc = bass.Bass("TRN2", target_bir_lowering=False, debug=False,
                   detect_race_conditions=detect_races)

    x_q = nc.dram_tensor("x_q", [TLOC, E], F32, kind="ExternalInput")
    x_kv = nc.dram_tensor("x_kv", [TLOC, E], F32, kind="ExternalInput")
    Wq = nc.dram_tensor("Wq", [E, E], F32, kind="ExternalInput")
    Wk = nc.dram_tensor("Wk", [E, E], F32, kind="ExternalInput")
    Wv = nc.dram_tensor("Wv", [E, E], F32, kind="ExternalInput")
    Wo = nc.dram_tensor("Wo", [E, E], F32, kind="ExternalInput")
    fac = nc.dram_tensor("fac", [E, E], F32, kind="ExternalInput")
    fbias = nc.dram_tensor("fbias", [1, E], F32, kind="ExternalInput")
    lamv = nc.dram_tensor("lamv", [4, 128], F32, kind="ExternalInput")
    gvec = nc.dram_tensor("gvec", [1, 256], F32, kind="ExternalInput")
    id16 = nc.dram_tensor("id16", [128, 128], BF16, kind="ExternalInput")
    bmask = nc.dram_tensor("bmask", [8 * 128, 128], BF16, kind="ExternalInput")

    out = nc.dram_tensor("out", [TLOC, E], F32, kind="ExternalOutput")

    ag_in = nc.dram_tensor("ag_in", [RSZ], BF16)
    ag_out = nc.dram_tensor("ag_out", [NC * RSZ], BF16)
    ar_in = nc.dram_tensor("ar_in", [E, E], F32)
    ar_out = nc.dram_tensor("ar_out", [E, E], F32)
    bscr = nc.dram_tensor("bscr", [128], F32)

    with ExitStack() as top:
        en = top.enter_context

        sems = {}
        cnt = {}
        for name in ("st", "se", "sv", "scc", "smisc", "sar", "sgp",
                     "sx0", "sx1", "sw0", "sw1", "svs0", "svs1",
                     "sk0", "sk1", "svh0", "svh1", "so0", "so1"):
            sems[name] = en(nc.semaphore(name))
            cnt[name] = 0

        def inc(instr, which, n=1):
            instr.then_inc(sems[which], n)
            cnt[which] += n
            return cnt[which]

        def val(which):
            return cnt[which]

        id16sb = en(nc.sbuf_tensor("id16sb", [128, 128], BF16))
        masks = en(nc.sbuf_tensor("masks", [128, 8 * 128], BF16))
        ones_c = en(nc.sbuf_tensor("ones_c", [128, 1], BF16))
        ones_r32 = en(nc.sbuf_tensor("ones_r32", [1, 128], F32))
        g_bc = en(nc.sbuf_tensor("g_bc", [128, 256], F32))
        lam_bc = en(nc.sbuf_tensor("lam_bc", [128, 1], F32))
        negM = en(nc.sbuf_tensor("negM", [128, 32], F32))
        qm2 = en(nc.sbuf_tensor("qm2", [1, 32], F32))
        kg = en(nc.sbuf_tensor("kg", [1, 16], F32))
        kmx_all = en(nc.sbuf_tensor("kmx_all", [1, 128], BF16))
        smalls = en(nc.sbuf_tensor("smalls", [1, 160], F32))
        st_m = en(nc.sbuf_tensor("st_m", [128, 1], F32))
        st_ss = en(nc.sbuf_tensor("st_ss", [128, 1], F32))
        st_v1 = en(nc.sbuf_tensor("st_v1", [128, 1], F32))
        st_v2 = en(nc.sbuf_tensor("st_v2", [128, 1], F32))
        st_r = en(nc.sbuf_tensor("st_r", [128, 1], F32))
        fin_sc = en(nc.sbuf_tensor("fin_sc", [128, 8], F32))
        grow = en(nc.sbuf_tensor("grow", [1, 256], F32))
        lrow = en(nc.sbuf_tensor("lrow", [1, 512], F32))
        o_sb = en(nc.sbuf_tensor("o_sb", [128, NT_LOC * E], BF16))

        qn_done_ref = [0]
        AG_DONE, AR_DONE = 1, 2

        def emit_ln_tiles(blk, src_dram, xb, hb, dump, dst_hT, tpps,
                          gate_st, gate_sv, id_gate):
            apply_done = [0] * NT_LOC
            x_done = [0] * NT_LOC
            tp_done = {}
            ev_done = {}
            for i in range(NT_LOC):
                b = i % 2

                @blk.sync
                def _(sync, i=i, b=b):
                    if i >= 2:
                        sync.wait_ge(sems["sv"], apply_done[i - 2])
                    elif gate_sv:
                        sync.wait_ge(sems["sv"], gate_sv)
                    x_done[i] = inc(sync.dma_start(
                        out=xb[b][:, :],
                        in_=bass.AP(src_dram, i * 128 * E, [[E, 128], [1, E]])),
                        f"sx{b}", 16)

                @blk.vector
                def _(v, i=i, b=b):
                    v.wait_ge(sems[f"sx{b}"], x_done[i])
                    rg = inc(v.reduce_sum(st_m[:, :], xb[b][:, :], axis=AX,
                                          opt_output=False), "sv")
                    v.wait_ge(sems["sv"], rg)
                    inc(v.tensor_scalar_mul(st_m[:, :], st_m[:, :], 1.0 / E), "sv")
                m_done = val("sv")

                @blk.scalar
                def _(s, i=i, b=b):
                    s.wait_ge(sems[f"sx{b}"], x_done[i])
                    inc(s.activation(dump[:, :], xb[b][:, :], AF.Square,
                                     accum_out=st_ss[:, :]), "se")
                ss_done = val("se")

                @blk.vector
                def _(v, i=i, m_done=m_done):
                    v.wait_ge(sems["se"], ss_done)
                    v.wait_ge(sems["sv"], m_done)
                    v.tensor_scalar(st_v1[:, :], st_ss[:, :], 1.0 / E, EPS,
                                    op0=ALU.mult, op1=ALU.add)
                    vw = inc(v.tensor_tensor(st_v2[:, :], st_m[:, :],
                                             st_m[:, :], op=ALU.mult), "sv")
                    v.wait_ge(sems["sv"], vw)
                    inc(v.tensor_tensor(st_v1[:, :], st_v1[:, :], st_v2[:, :],
                                        op=ALU.subtract), "sv")
                var_done = val("sv")

                @blk.scalar
                def _(s, i=i):
                    s.wait_ge(sems["sv"], var_done)
                    lw = inc(s.activation(st_r[:, :], st_v1[:, :], AF.Ln,
                                          bias=0.0, scale=1.0), "se")
                    s.wait_ge(sems["se"], lw)
                    inc(s.activation(st_r[:, :], st_r[:, :], AF.Exp,
                                     bias=0.0, scale=-0.5), "se")
                r_done = val("se")

                @blk.vector
                def _(v, i=i, b=b, m_done=m_done):
                    v.wait_ge(sems["se"], r_done)
                    v.wait_ge(sems["sv"], m_done)
                    if i >= 1:
                        v.wait_ge(sems["st"], tp_done[(i - 1, 1)])
                    apply_done[i] = inc(v.tensor_scalar(
                        hb[b][:, :], xb[b][:, :], st_m[:, :], st_r[:, :],
                        op0=ALU.subtract, op1=ALU.mult), "sv")

                for hchunk in range(2):
                    k = i * 2 + hchunk
                    quad = tpps[0:4] if k % 2 == 0 else tpps[4:8]

                    @blk.tensor
                    def _(t, i=i, b=b, hchunk=hchunk, k=k, quad=quad):
                        t.wait_ge(sems["sv"], apply_done[i])
                        if k >= 2:
                            t.wait_ge(sems["sv"], ev_done[k - 2])
                        else:
                            t.wait_ge(sems["smisc"], id_gate)
                            if gate_st:
                                t.wait_ge(sems["st"], gate_st)
                            if gate_sv:
                                t.wait_ge(sems["sv"], gate_sv)
                        last = None
                        for c8 in range(8):
                            ch = hchunk * 8 + c8
                            last = t.matmul(
                                quad[c8 % 4][:, (c8 // 4) * 128:(c8 // 4 + 1) * 128],
                                hb[b][:, ch * 128:(ch + 1) * 128],
                                id16sb[:, :], start=True, stop=True)
                        tp_done[(i, hchunk)] = inc(last, "st")

                    @blk.vector
                    def _(v, i=i, hchunk=hchunk, k=k, quad=quad):
                        v.wait_ge(sems["st"], tp_done[(i, hchunk)])
                        ins = None
                        for par in range(4):
                            ins = v.tensor_copy(
                                bass.AP(dst_hT,
                                        (hchunk * 8 + par) * 1024 + i * 128,
                                        [[NCH * TLOC, 128], [4096, 2], [1, 128]]),
                                quad[par][:, 0:256])
                        ev_done[k] = inc(ins, "sv")
            return ev_done[2 * NT_LOC - 1]

        p12 = ExitStack()
        if True:
            qT = p12.enter_context(nc.sbuf_tensor("qT", [128, NCH * TLOC], BF16))

            with ExitStack() as p1:
                e1 = p1.enter_context
                xb = [e1(nc.sbuf_tensor(f"xb{i}", [128, E], F32)) for i in range(2)]
                hb = [e1(nc.sbuf_tensor(f"hb{i}", [128, E], BF16)) for i in range(2)]
                dump = e1(nc.sbuf_tensor("dump", [128, E], BF16))
                hT = e1(nc.sbuf_tensor("hT", [128, NCH * TLOC], BF16))
                wfh = [e1(nc.sbuf_tensor(f"wfh{i}", [128, 16 * 512], BF16))
                       for i in range(2)]
                kT_own = e1(nc.sbuf_tensor("kT_own", [128, NCH * TLOC], BF16))
                vstg = [e1(nc.sbuf_tensor(f"vstg{i}", [128, 4 * 2 * 258], BF16))
                        for i in range(2)]
                sqb = e1(nc.sbuf_tensor("sqb", [128, 1024], BF16))
                kmx_own = e1(nc.sbuf_tensor("kmx_own", [1, 16], F32))
                kmx_own16 = e1(nc.sbuf_tensor("kmx_own16", [1, 16], BF16))

                blk0 = e1(nc.Block())

                @blk0.gpsimd
                def _(g):
                    g.memset(ones_c[:, :], 1.0)
                    g.memset(ones_r32[:1, :], 1.0)
                    last = None
                    for vb in range(2):
                        g.memset(bass.AP(vstg[vb], 256,
                                         [[4 * 2 * 258, 128], [258, 8], [1, 1]]),
                                 1.0)
                        last = g.memset(bass.AP(vstg[vb], 257,
                                                [[4 * 2 * 258, 128], [258, 8],
                                                 [1, 1]]),
                                        0.0)
                    inc(last, "sgp")
                    inc(g.dma_start(out=ar_in.ap(), in_=fac.ap()), "sar", 16)
                memset_done = 1
                arb_done = val("sar")

                @blk0.sync
                def _(sync):
                    inc(sync.dma_start(out=id16sb[:, :], in_=id16.ap()), "smisc", 16)
                    inc(sync.dma_start(
                        out=masks[:, :],
                        in_=bass.AP(bmask, 0, [[128, 128], [128 * 128, 8], [1, 128]])),
                        "smisc", 16)
                    inc(sync.dma_start(out=grow[:1, :], in_=gvec.ap()), "smisc", 16)
                    for rr in range(4):
                        inc(sync.dma_start(
                            out=lrow[0:1, rr * 128:(rr + 1) * 128],
                            in_=bass.AP(lamv, rr * 128, [[128, 1], [1, 128]])),
                            "smisc", 16)
                const_ld = val("smisc")

                id_gate = const_ld

                @blk0.scalar
                def _(s):
                    s.wait_ge(sems["sgp"], memset_done)
                    s.activation(st_v2[:, :], ones_c[:, :], AF.Square,
                                 accum_out=st_v1[:, :])
                    s.activation(st_v2[:, :], ones_c[:, :], AF.Ln,
                                 bias=0.0, scale=1.0)
                    s.activation(st_v2[:, :], ones_c[:, :], AF.Exp,
                                 bias=0.0, scale=1.0)

                @blk0.vector
                def _(v):
                    v.wait_ge(sems["sgp"], memset_done)
                    v.reciprocal(st_v2[:, :], ones_c[:, :])

                with ExitStack() as tpsc:
                    tpps = [tpsc.enter_context(
                        nc.psum_tensor(f"tpA{i}", [128, 512], F32))
                        for i in range(8)]
                    kv_hT_done = emit_ln_tiles(blk0, x_kv, xb, hb, dump, hT,
                                               tpps, 0, 0, id_gate)

                def w_qtr_loads(wsrc, qtr, wait_st=0):
                    sem = f"sw{qtr % 2}"

                    @blk0.gpsimd
                    def _(g):
                        if wait_st:
                            g.wait_ge(sems["st"], wait_st)
                        for din in range(16):
                            inc(g.dma_start(
                                out=wfh[qtr % 2][:, din * 512:(din + 1) * 512],
                                in_=bass.AP(wsrc, din * 128 * E + qtr * 512,
                                            [[E, 128], [1, 512]])), sem, 16)
                    return (sem, val(sem))

                with ExitStack() as accsc:
                    pacc0 = accsc.enter_context(
                        nc.psum_tensor("pacc0", [128, 2048], F32))
                    pacc1 = accsc.enter_context(
                        nc.psum_tensor("pacc1", [128, 2048], F32))

                    def proj_fm(hT_ready_sv, wsrc, w_first_gate, dst,
                                dst_scale):
                        evac_done = {}
                        qtr_use = {}
                        rot = 0
                        for qtr in range(4):
                            gate = qtr_use[qtr - 2] if qtr >= 2 else w_first_gate
                            wdone = w_qtr_loads(wsrc, qtr, wait_st=gate)
                            for grp in range(2):
                                accs = pacc0 if rot % 2 == 0 else pacc1
                                myrot = rot
                                rot += 1

                                @blk0.tensor
                                def _(t, qtr=qtr, grp=grp, accs=accs,
                                      myrot=myrot, wdone=wdone):
                                    t.wait_ge(sems["sv"], hT_ready_sv)
                                    if myrot >= 2:
                                        t.wait_ge(sems["sv"],
                                                  evac_done[myrot - 2])
                                    t.wait_ge(sems[wdone[0]], wdone[1])
                                    last = None
                                    for din in range(16):
                                        for i in range(4):
                                            last = t.matmul(
                                                accs[:, i * 512:(i + 1) * 512],
                                                wfh[qtr % 2][:, din * 512 + i * 128:
                                                             din * 512 + (i + 1) * 128],
                                                hT[:, din * 1024 + grp * 512:
                                                   din * 1024 + (grp + 1) * 512],
                                                start=(din == 0),
                                                stop=(din == 15))
                                    inc(last, "st")
                                mm_done = val("st")
                                qtr_use[qtr] = mm_done

                                @blk0.vector
                                def _(v, qtr=qtr, grp=grp, accs=accs,
                                      myrot=myrot, mm_done=mm_done):
                                    v.wait_ge(sems["st"], mm_done)
                                    ins = None
                                    for i in range(4):
                                        dout = qtr * 4 + i
                                        col = dout * 1024 + grp * 512
                                        if dst_scale is None:
                                            ins = v.tensor_copy(
                                                dst[:, col:col + 512],
                                                accs[:, i * 512:(i + 1) * 512])
                                        else:
                                            ins = v.tensor_scalar_mul(
                                                dst[:, col:col + 512],
                                                accs[:, i * 512:(i + 1) * 512],
                                                dst_scale)
                                    evac_done[myrot] = inc(ins, "sv")
                        return val("sv"), qtr_use[3]

                    def emit_norms(src, ready_sv, put):
                        prev_t = 0
                        prev_v = 0
                        for hc in range(16):
                            @blk0.scalar
                            def _(s, hc=hc, prev_t=prev_t):
                                if hc == 0:
                                    s.wait_ge(sems["sv"], ready_sv)
                                else:
                                    s.wait_ge(sems["st"], prev_t)
                                inc(s.activation(
                                    sqb[:, :], src[:, hc * 1024:(hc + 1) * 1024],
                                    AF.Square), "se")
                            sq_done = val("se")

                            @blk0.tensor
                            def _(t, hc=hc, sq_done=sq_done, prev_v=prev_v):
                                t.wait_ge(sems["se"], sq_done)
                                if prev_v:
                                    t.wait_ge(sems["sv"], prev_v)
                                t.matmul(pacc0[0:1, 0:512], ones_c[:, :],
                                         sqb[:, 0:512], start=True, stop=True)
                                inc(t.matmul(pacc0[0:1, 512:1024], ones_c[:, :],
                                             sqb[:, 512:1024], start=True,
                                             stop=True), "st")
                            prev_t = val("st")

                            @blk0.vector
                            def _(v, hc=hc, prev_t=prev_t):
                                v.wait_ge(sems["st"], prev_t)
                                inc(put(v, hc), "sv")
                            prev_v = val("sv")
                        return val("sv")

                    k_done, k_wlast = proj_fm(kv_hT_done, Wk, 0, kT_own, None)

                    def put_k(v, hc):
                        return v.reduce_max(kmx_own[:1, hc:hc + 1],
                                            pacc0[0:1, 0:1024], axis=AX, opt_output=False)
                    knorm_done = emit_norms(kT_own, k_done, put_k)

                    @blk0.vector
                    def _(v):
                        v.wait_ge(sems["sv"], knorm_done)
                        inc(v.tensor_copy(kmx_own16[:1, :], kmx_own[:1, :]), "sv")
                    kmx16_done = val("sv")

                    @blk0.sync
                    def _(sync):
                        sync.wait_ge(sems["sv"], k_done)
                        for q in range(4):
                            inc(sync.dma_start(
                                out=bass.AP(ag_in, KOFF + q * 4 * 128 * 1024,
                                            [[1024, 128], [128 * 1024, 4],
                                             [1, 1024]]),
                                in_=bass.AP(kT_own, q * 4 * 1024,
                                            [[NCH * TLOC, 128], [1024, 4],
                                             [1, 1024]])), "smisc", 16)
                        sync.wait_ge(sems["sv"], kmx16_done)
                        inc(sync.dma_start(
                            out=bass.AP(ag_in, SOFF, [[16, 1], [1, 16]]),
                            in_=kmx_own16[:1, :]), "smisc", 16)
                    kstage_done = val("smisc")

                    v_evac = {}
                    v_dma = {}
                    v_qtr_use = {}
                    vrot = 0
                    for qtr in range(4):
                        gate = v_qtr_use[qtr - 2] if qtr >= 2 else k_wlast
                        wv_done = w_qtr_loads(Wv, qtr, wait_st=gate)
                        for blki in range(2):
                            accs = pacc0 if vrot % 2 == 0 else pacc1
                            myrot = vrot
                            vrot += 1

                            @blk0.tensor
                            def _(t, qtr=qtr, blki=blki, accs=accs,
                                  myrot=myrot, wv_done=wv_done):
                                t.wait_ge(sems["sv"], kv_hT_done)
                                if myrot == 0:
                                    t.wait_ge(sems["sv"], knorm_done)
                                if myrot >= 2:
                                    t.wait_ge(sems["sv"], v_evac[myrot - 2])
                                t.wait_ge(sems[wv_done[0]], wv_done[1])
                                last = None
                                for din in range(16):
                                    for i in range(4):
                                        tile = blki * 4 + i
                                        last = t.matmul(
                                            accs[:, i * 512:(i + 1) * 512],
                                            hT[:, din * 1024 + tile * 128:
                                               din * 1024 + (tile + 1) * 128],
                                            wfh[qtr % 2][:, din * 512:(din + 1) * 512],
                                            start=(din == 0), stop=(din == 15))
                                inc(last, "st")
                            vm_done = val("st")
                            v_qtr_use[qtr] = vm_done

                            @blk0.vector
                            def _(v, accs=accs, myrot=myrot, vm_done=vm_done):
                                v.wait_ge(sems["st"], vm_done)
                                v.wait_ge(sems["sgp"], memset_done)
                                if myrot >= 2:
                                    v.wait_ge(sems[f"svs{myrot % 2}"],
                                              v_dma[myrot - 2])
                                inc(v.tensor_copy(
                                    bass.AP(vstg[myrot % 2], 0,
                                            [[4 * 2 * 258, 128], [516, 4],
                                             [258, 2], [1, 256]]),
                                    bass.AP(accs, 0,
                                            [[2048, 128], [512, 4],
                                             [256, 2], [1, 256]])), "sv")
                            v_evac[myrot] = val("sv")

                            @blk0.sync
                            def _(sync, qtr=qtr, blki=blki, myrot=myrot):
                                sync.wait_ge(sems["sv"], v_evac[myrot])
                                h0 = qtr * 2
                                v_dma[myrot] = inc(sync.dma_start(
                                    out=bass.AP(ag_in,
                                                VOFF + blki * 4 * TILE_V + h0 * 258,
                                                [[VROW, 128], [TILE_V, 4],
                                                 [258, 2], [1, 258]]),
                                    in_=bass.AP(vstg[myrot % 2], 0,
                                                [[4 * 2 * 258, 128], [516, 4],
                                                 [258, 2], [1, 258]])),
                                    f"svs{myrot % 2}", 16)
                    vstage0 = val("svs0")
                    vstage1 = val("svs1")

                    @blk0.gpsimd
                    def _(g):
                        g.wait_ge(sems["smisc"], kstage_done)
                        g.wait_ge(sems["svs0"], vstage0)
                        g.wait_ge(sems["svs1"], vstage1)
                        inc(g.collective_compute(
                            "AllGather", ALU.bypass,
                            replica_groups=[list(range(NC))],
                            ins=[ag_in.ap()], outs=[ag_out.ap()]), "scc")
                        g.wait_ge(sems["sar"], arb_done)
                        inc(g.collective_compute(
                            "AllReduce", ALU.add,
                            replica_groups=[list(range(NC))],
                            ins=[ar_in.ap()], outs=[ar_out.ap()]), "scc")

                    q_gate_st = val("st")
                    q_gate_sv = val("sv")

                with ExitStack() as tpsc2:
                    tpps2 = [tpsc2.enter_context(
                        nc.psum_tensor(f"tpB{i}", [128, 512], F32))
                        for i in range(8)]
                    q_hT_done = emit_ln_tiles(blk0, x_q, xb, hb, dump, hT, tpps2,
                                              q_gate_st, q_gate_sv, id_gate)

                with ExitStack() as accsc2:
                    pacc0 = accsc2.enter_context(
                        nc.psum_tensor("pacc0b", [128, 2048], F32))
                    pacc1 = accsc2.enter_context(
                        nc.psum_tensor("pacc1b", [128, 2048], F32))

                    def proj_fm2(hT_ready_sv, wsrc, w_first_gate, dst,
                                 dst_scale):
                        evac_done = {}
                        qtr_use = {}
                        rot = 0
                        for qtr in range(4):
                            gate = qtr_use[qtr - 2] if qtr >= 2 else w_first_gate
                            wdone = w_qtr_loads(wsrc, qtr, wait_st=gate)
                            for grp in range(2):
                                accs = pacc0 if rot % 2 == 0 else pacc1
                                myrot = rot
                                rot += 1

                                @blk0.tensor
                                def _(t, qtr=qtr, grp=grp, accs=accs,
                                      myrot=myrot, wdone=wdone):
                                    t.wait_ge(sems["sv"], hT_ready_sv)
                                    if myrot >= 2:
                                        t.wait_ge(sems["sv"],
                                                  evac_done[myrot - 2])
                                    t.wait_ge(sems[wdone[0]], wdone[1])
                                    last = None
                                    for din in range(16):
                                        for i in range(4):
                                            last = t.matmul(
                                                accs[:, i * 512:(i + 1) * 512],
                                                wfh[qtr % 2][:, din * 512 + i * 128:
                                                             din * 512 + (i + 1) * 128],
                                                hT[:, din * 1024 + grp * 512:
                                                   din * 1024 + (grp + 1) * 512],
                                                start=(din == 0),
                                                stop=(din == 15))
                                    inc(last, "st")
                                mm_done = val("st")
                                qtr_use[qtr] = mm_done

                                @blk0.vector
                                def _(v, qtr=qtr, grp=grp, accs=accs,
                                      myrot=myrot, mm_done=mm_done):
                                    v.wait_ge(sems["st"], mm_done)
                                    ins = None
                                    for i in range(4):
                                        dout = qtr * 4 + i
                                        col = dout * 1024 + grp * 512
                                        ins = v.tensor_scalar_mul(
                                            dst[:, col:col + 512],
                                            accs[:, i * 512:(i + 1) * 512],
                                            dst_scale)
                                    evac_done[myrot] = inc(ins, "sv")
                        return val("sv")

                    q_done = proj_fm2(q_hT_done, Wq, val("st"), qT, SCALE)

                    def emit_norms2(src, ready_sv, put):
                        prev_t = 0
                        prev_v = 0
                        for hc in range(16):
                            @blk0.scalar
                            def _(s, hc=hc, prev_t=prev_t):
                                if hc == 0:
                                    s.wait_ge(sems["sv"], ready_sv)
                                else:
                                    s.wait_ge(sems["st"], prev_t)
                                inc(s.activation(
                                    sqb[:, :], src[:, hc * 1024:(hc + 1) * 1024],
                                    AF.Square), "se")
                            sq_done = val("se")

                            @blk0.tensor
                            def _(t, hc=hc, sq_done=sq_done, prev_v=prev_v):
                                t.wait_ge(sems["se"], sq_done)
                                if prev_v:
                                    t.wait_ge(sems["sv"], prev_v)
                                t.matmul(pacc0[0:1, 0:512], ones_c[:, :],
                                         sqb[:, 0:512], start=True, stop=True)
                                inc(t.matmul(pacc0[0:1, 512:1024], ones_c[:, :],
                                             sqb[:, 512:1024], start=True,
                                             stop=True), "st")
                            prev_t = val("st")

                            @blk0.vector
                            def _(v, hc=hc, prev_t=prev_t):
                                v.wait_ge(sems["st"], prev_t)
                                inc(put(v, hc), "sv")
                            prev_v = val("sv")
                        return val("sv")

                    def put_q(v, hc):
                        v.reduce_max(qm2[:1, hc:hc + 1], pacc0[0:1, 0:512],
                                     axis=AX, opt_output=False)
                        return v.reduce_max(qm2[:1, 16 + hc:17 + hc],
                                            pacc0[0:1, 512:1024], axis=AX, opt_output=False)
                    qn_done_ref[0] = emit_norms2(qT, q_done, put_q)

                    @blk0.vector
                    def _(v):
                        v.wait_ge(sems["smisc"], const_ld)
                        lm1 = inc(v.tensor_tensor(smalls[:1, 0:128],
                                                  lrow[:1, 0:128],
                                                  lrow[:1, 128:256],
                                                  op=ALU.mult), "sv")
                        v.wait_ge(sems["sv"], lm1)
                        rg = inc(v.reduce_sum(smalls[:1, 130:131],
                                              smalls[:1, 0:128],
                                              axis=AX, opt_output=False), "sv")
                        v.wait_ge(sems["sv"], rg)
                        lm2 = inc(v.tensor_tensor(smalls[:1, 0:128],
                                                  lrow[:1, 256:384],
                                                  lrow[:1, 384:512],
                                                  op=ALU.mult), "sv")
                        v.wait_ge(sems["sv"], lm2)
                        inc(v.reduce_sum(smalls[:1, 131:132], smalls[:1, 0:128],
                                         axis=AX, opt_output=False), "sv")
                    lsum_done = val("sv")

                    @blk0.scalar
                    def _(s):
                        s.wait_ge(sems["sv"], lsum_done)
                        inc(s.activation(smalls[:1, 133:135], smalls[:1, 130:132],
                                         AF.Exp), "se")
                    lexp_done = val("se")

                    @blk0.vector
                    def _(v):
                        v.wait_ge(sems["se"], lexp_done)
                        inc(v.scalar_tensor_tensor(smalls[:1, 136:137],
                                                   smalls[:1, 134:135],
                                                   -LAMBDA_INIT,
                                                   smalls[:1, 133:134],
                                                   op0=ALU.add,
                                                   op1=ALU.subtract), "sv")
                    neglam_done = val("sv")

                    @blk0.sync
                    def _(sync):
                        sync.wait_ge(sems["sv"], neglam_done)
                        d1 = inc(sync.dma_start(
                            out=bass.AP(bscr, 40, [[1, 1]]),
                            in_=smalls[:1, 136:137]), "smisc", 16)
                        sync.wait_ge(sems["smisc"], d1)
                        inc(sync.dma_start(
                            out=lam_bc[:, :],
                            in_=bass.AP(bscr, 40, [[0, 128], [1, 1]])),
                            "smisc", 16)
                        inc(sync.dma_start(
                            out=g_bc[:, :],
                            in_=bass.AP(gvec, 0, [[0, 128], [1, 256]])),
                            "smisc", 16)
                    lamg_dma = val("smisc")

                    @blk0.vector
                    def _(v):
                        v.wait_ge(sems["smisc"], lamg_dma)
                        inc(v.tensor_scalar_mul(g_bc[:, :], g_bc[:, :],
                                                1.0 - LAMBDA_INIT), "sv")

            with ExitStack() as p2:
                e2 = p2.enter_context
                kT = [e2(nc.sbuf_tensor(f"kT{i}", [128, T], BF16))
                      for i in range(2)]
                vh = [e2(nc.sbuf_tensor(f"vh{i}", [128, 64 * 258], BF16))
                      for i in range(2)]
                aT = [e2(nc.sbuf_tensor(f"aT{i}", [128, 1024], BF16))
                      for i in range(2)]
                t0b = e2(nc.sbuf_tensor("t0b", [128, 8 * 256], F32))
                dump2 = e2(nc.sbuf_tensor("dump2", [128, 256], BF16))
                Oacc = [e2(nc.psum_tensor(f"Oacc{i}", [128, 512], F32))
                        for i in range(4)]
                Sps = [e2(nc.psum_tensor(f"Sps{i}", [128, 1024], F32))
                       for i in range(2)]

                blk1 = e2(nc.Block())

                @blk1.sync
                def _(sync):
                    sync.wait_ge(sems["scc"], AG_DONE)
                    inc(sync.dma_start(
                        out=kmx_all[:1, :],
                        in_=bass.AP(ag_out, SOFF,
                                    [[128, 1], [1, 16], [RSZ, 8]]),
                        allow_slow_non_contiguous=True), "smisc", 16)
                kmx_ld = val("smisc")

                @blk1.vector
                def _(v):
                    v.wait_ge(sems["smisc"], kmx_ld)
                    last = None
                    for hc in range(16):
                        last = v.reduce_max(kg[:1, hc:hc + 1],
                                     kmx_all[:1, hc * 8:(hc + 1) * 8], axis=AX, opt_output=False)
                    rg = inc(last, "sv")
                    v.wait_ge(sems["sv"], rg)
                    v.wait_ge(sems["sv"], qn_done_ref[0])
                    v.scalar_tensor_tensor(smalls[:1, 0:16], qm2[:1, 0:16],
                                           1.0609, kg[:1, :],
                                           op0=ALU.mult, op1=ALU.mult)
                    inc(v.scalar_tensor_tensor(smalls[:1, 16:32],
                                               qm2[:1, 16:32], 1.0609,
                                               kg[:1, :],
                                               op0=ALU.mult, op1=ALU.mult),
                        "sv")
                m2_done = val("sv")

                @blk1.scalar
                def _(s):
                    s.wait_ge(sems["sv"], m2_done)
                    ln1 = inc(s.activation(smalls[:1, 32:64],
                                           smalls[:1, 0:32], AF.Ln), "se")
                    s.wait_ge(sems["se"], ln1)
                    inc(s.activation(smalls[:1, 32:64], smalls[:1, 32:64],
                                     AF.Exp, bias=0.0, scale=0.5),
                        "se")
                mexp_done = val("se")

                @blk1.vector
                def _(v):
                    v.wait_ge(sems["se"], mexp_done)
                    inc(v.tensor_scalar_mul(smalls[:1, 32:64],
                                            smalls[:1, 32:64], -1.0), "sv")
                negm_row = val("sv")

                @blk1.sync
                def _(sync):
                    sync.wait_ge(sems["sv"], negm_row)
                    d1 = inc(sync.dma_start(
                        out=bass.AP(bscr, 0, [[1, 32]]),
                        in_=smalls[:1, 32:64]), "smisc", 16)
                    sync.wait_ge(sems["smisc"], d1)
                    inc(sync.dma_start(
                        out=negM[:, :],
                        in_=bass.AP(bscr, 0, [[0, 128], [1, 32]])),
                        "smisc", 16)
                negm_dma = val("smisc")

                @blk1.vector
                def _(v):
                    v.wait_ge(sems["smisc"], negm_dma)
                    v.memset(Sps[0][:, :], 0.0)
                    inc(v.memset(Sps[1][:, :], 0.0), "sv")
                p2init = val("sv")

                ST_S = {}
                ST_AV = {}
                SE_E = {}
                gpair = 0
                sweep_evac_done = p2init
                hc_last_av = {}
                kv_k_done = {}
                kv_v_done = [0] * 8

                for h in range(8):
                    @blk1.sync
                    def _(sync, h=h):
                        buf = h % 2
                        for comp in range(2):
                            hc = 2 * h + comp
                            if hc >= 2 and (hc - 2) in hc_last_av:
                                sync.wait_ge(sems["st"], hc_last_av[hc - 2])
                            kv_k_done[hc] = inc(sync.dma_start(
                                out=kT[hc % 2][:, :],
                                in_=bass.AP(ag_out, KOFF + hc * 128 * 1024,
                                            [[1024, 128], [RSZ, 8], [1, 1024]])),
                                f"sk{hc % 2}", 16)
                        if h >= 2:
                            sync.wait_ge(sems["st"], hc_last_av[2 * h - 3])
                        for b in range(NC):
                            kv_v_done[h] = inc(sync.dma_start(
                                out=bass.AP(vh[buf], b * 8 * 258,
                                            [[64 * 258, 128], [258, 8],
                                             [1, 258]]),
                                in_=bass.AP(ag_out, b * RSZ + VOFF + h * 258,
                                            [[VROW, 128], [128 * VROW, 8],
                                             [1, 258]])), f"svh{h % 2}", 16)

                    for comp in range(2):
                        hc = 2 * h + comp
                        for grp in range(2):
                            maxj = 32 if grp == 0 else 64
                            pairs = []
                            j = 0
                            while j < maxj:
                                pairs.append((j, j + 1 if j + 1 < maxj else None))
                                j += 2
                            P = len(pairs)
                            base_g = gpair
                            negm_col = grp * 16 + hc

                            sv_base = val("sv")
                            mask_cum = {}
                            cum = 0
                            for pi, pr in enumerate(pairs):
                                for j in pr:
                                    if j is None:
                                        continue
                                    r = j // 8 - 4 * grp
                                    if 0 <= r <= 3:
                                        cum += 1
                                mask_cum[pi] = sv_base + cum

                            se_base = val("se")
                            for pi in range(P):
                                SE_E[base_g + pi] = se_base + pi + 1

                            @blk1.tensor
                            def _(t, h=h, grp=grp, comp=comp, hc=hc,
                                  pairs=pairs, base_g=base_g, P=P,
                                  sv_base=sv_base, mask_cum=mask_cum,
                                  gate_sv=sweep_evac_done):
                                def emit_S(pi):
                                    g = base_g + pi
                                    reg = g % 2
                                    if g - 2 >= 0:
                                        t.wait_ge(sems["se"], SE_E[g - 2])
                                    if pi == 0:
                                        t.wait_ge(sems[f"sk{hc % 2}"],
                                                  kv_k_done[hc])
                                        t.wait_ge(sems["sv"], p2init)
                                    last = None
                                    for idx, j in enumerate(pairs[pi]):
                                        if j is None:
                                            continue
                                        r0 = band_r0(grp, j)
                                        last = t.matmul(
                                            Sps[reg][:, idx * 512 + 128 * r0:
                                                     (idx + 1) * 512],
                                            kT[hc % 2][:, j * 128:(j + 1) * 128],
                                            qT[:, hc * 1024 + grp * 512 + 128 * r0:
                                               hc * 1024 + (grp + 1) * 512],
                                            start=True, stop=True)
                                    ST_S[g] = inc(last, "st")

                                def emit_AV(pi):
                                    g = base_g + pi
                                    reg = g % 2
                                    t.wait_ge(sems["se"], SE_E[g])
                                    prev = mask_cum[pi - 1] if pi > 0 else sv_base
                                    if mask_cum[pi] > prev:
                                        t.wait_ge(sems["sv"], mask_cum[pi])
                                    if pi == 0:
                                        t.wait_ge(sems["sv"], gate_sv)
                                        t.wait_ge(sems[f"svh{h % 2}"],
                                                  kv_v_done[h])
                                    last = None
                                    for idx, j in enumerate(pairs[pi]):
                                        if j is None:
                                            continue
                                        r0 = band_r0(grp, j)
                                        for r in range(r0, 4):
                                            last = t.matmul(
                                                Oacc[r][:, 0:257],
                                                aT[reg][:, idx * 512 + r * 128:
                                                        idx * 512 + (r + 1) * 128],
                                                vh[h % 2][:, j * 258:j * 258 + 257],
                                                start=(j == 0),
                                                stop=(j == slot_last_j(grp, r)))
                                    ST_AV[g] = inc(last, "st")

                                emit_S(0)
                                for pi in range(1, P):
                                    emit_S(pi)
                                    emit_AV(pi - 1)
                                emit_AV(P - 1)

                            @blk1.scalar
                            def _(s, grp=grp, pairs=pairs, base_g=base_g,
                                  P=P, negm_col=negm_col):
                                for pi in range(P):
                                    g = base_g + pi
                                    j0, j1 = pairs[pi]
                                    reg = g % 2
                                    s.wait_ge(sems["st"], ST_S[g])
                                    if g - 2 >= 0:
                                        s.wait_ge(sems["st"], ST_AV[g - 2])
                                    r0p = band_r0(grp, j0)
                                    w = (1024 if j1 is not None else 512) \
                                        - 128 * r0p
                                    got = inc(s.activation(
                                        aT[reg][:, 128 * r0p:128 * r0p + w],
                                        Sps[reg][:, 128 * r0p:128 * r0p + w],
                                        AF.Exp,
                                        bias=negM[:, negm_col:negm_col + 1],
                                        scale=1.0), "se")
                                    assert got == SE_E[g]

                            @blk1.vector
                            def _(v, grp=grp, pairs=pairs, base_g=base_g,
                                  mask_cum=mask_cum):
                                for pi, pr in enumerate(pairs):
                                    g = base_g + pi
                                    reg = g % 2
                                    for idx, j in enumerate(pr):
                                        if j is None:
                                            continue
                                        r = j // 8 - 4 * grp
                                        if r < 0 or r > 3:
                                            continue
                                        v.wait_ge(sems["se"], SE_E[g])
                                        inc(v.tensor_tensor(
                                            aT[reg][:, idx * 512 + r * 128:
                                                    idx * 512 + (r + 1) * 128],
                                            aT[reg][:, idx * 512 + r * 128:
                                                    idx * 512 + (r + 1) * 128],
                                            masks[:, (j % 8) * 128:
                                                  (j % 8 + 1) * 128],
                                            op=ALU.mult), "sv")
                                assert val("sv") == mask_cum[len(pairs) - 1]

                            gpair += P
                            last_g = gpair - 1

                            if comp == 0:
                                @blk1.vector
                                def _(v, grp=grp, last_g=last_g):
                                    v.wait_ge(sems["st"], ST_AV[last_g])
                                    last = None
                                    for r in range(4):
                                        last = v.reciprocal(fin_sc[:, r:r + 1],
                                                            Oacc[r][:, 256:257])
                                    rg = inc(last, "sv")
                                    v.wait_ge(sems["sv"], rg)
                                    for r in range(4):
                                        last = v.tensor_scalar_mul(
                                            t0b[:, (grp * 4 + r) * 256:
                                                (grp * 4 + r + 1) * 256],
                                            Oacc[r][:, 0:256],
                                            fin_sc[:, r:r + 1])
                                    inc(last, "sv")
                                sweep_evac_done = val("sv")
                            else:
                                fin_v1 = {}

                                @blk1.vector
                                def _(v, grp=grp, last_g=last_g,
                                      fin_v1=fin_v1):
                                    v.wait_ge(sems["st"], ST_AV[last_g])
                                    last = None
                                    for r in range(4):
                                        last = v.reciprocal(fin_sc[:, r:r + 1],
                                                            Oacc[r][:, 256:257])
                                    rg = inc(last, "sv")
                                    v.wait_ge(sems["sv"], rg)
                                    last = None
                                    for r in range(4):
                                        last = v.tensor_tensor(
                                            fin_sc[:, r:r + 1],
                                            fin_sc[:, r:r + 1],
                                            lam_bc[:, :], op=ALU.mult)
                                    lm = inc(last, "sv")
                                    v.wait_ge(sems["sv"], lm)
                                    for r in range(4):
                                        sl = (grp * 4 + r) * 256
                                        fin_v1[r] = inc(
                                            v.scalar_tensor_tensor(
                                                t0b[:, sl:sl + 256],
                                                Oacc[r][:, 0:256],
                                                fin_sc[:, r:r + 1],
                                                t0b[:, sl:sl + 256],
                                                op0=ALU.mult, op1=ALU.add),
                                            "sv")

                                se_sq = {}

                                @blk1.scalar
                                def _(s, grp=grp, fin_v1=fin_v1,
                                      se_sq=se_sq):
                                    for r in range(4):
                                        sl = (grp * 4 + r) * 256
                                        s.wait_ge(sems["sv"], fin_v1[r])
                                        se_sq[r] = inc(s.activation(
                                            dump2[:, 0:256],
                                            t0b[:, sl:sl + 256],
                                            AF.Square,
                                            accum_out=fin_sc[:, 4 + r:5 + r]),
                                            "se")

                                sv_eps = {}

                                @blk1.vector
                                def _(v, se_sq=se_sq, sv_eps=sv_eps):
                                    for r in range(4):
                                        v.wait_ge(sems["se"], se_sq[r])
                                        sv_eps[r] = inc(v.tensor_scalar(
                                            fin_sc[:, 4 + r:5 + r],
                                            fin_sc[:, 4 + r:5 + r],
                                            1.0 / 256.0, EPS,
                                            op0=ALU.mult, op1=ALU.add), "sv")

                                se_fin = {}

                                @blk1.scalar
                                def _(s, sv_eps=sv_eps, se_fin=se_fin):
                                    for r in range(4):
                                        s.wait_ge(sems["sv"], sv_eps[r])
                                        lw = inc(s.activation(
                                            fin_sc[:, 4 + r:5 + r],
                                            fin_sc[:, 4 + r:5 + r],
                                            AF.Ln, bias=0.0, scale=1.0), "se")
                                        s.wait_ge(sems["se"], lw)
                                        se_fin[r] = inc(s.activation(
                                            fin_sc[:, 4 + r:5 + r],
                                            fin_sc[:, 4 + r:5 + r],
                                            AF.Exp, bias=0.0, scale=-0.5),
                                            "se")

                                @blk1.vector
                                def _(v, h=h, grp=grp, se_fin=se_fin):
                                    for r in range(4):
                                        sl = (grp * 4 + r) * 256
                                        v.wait_ge(sems["se"], se_fin[r])
                                        lt = grp * 4 + r
                                        inc(v.scalar_tensor_tensor(
                                            o_sb[:, lt * E + h * 256:
                                                 lt * E + (h + 1) * 256],
                                            t0b[:, sl:sl + 256],
                                            fin_sc[:, 4 + r:5 + r],
                                            g_bc[:, :],
                                            op0=ALU.mult, op1=ALU.mult), "sv")
                                sweep_evac_done = val("sv")
                        hc_last_av[hc] = ST_AV[gpair - 1]

            p12.close()

            with ExitStack() as p3:
                e3 = p3.enter_context
                oT = e3(nc.sbuf_tensor("oT", [128, NCH * TLOC], BF16))
                x1 = e3(nc.sbuf_tensor("x1", [128, NT_LOC * E], F32))
                bias_bc = e3(nc.sbuf_tensor("bias_bc", [128, E], F32))
                wsl = [e3(nc.sbuf_tensor(f"wsl{i}", [128, 16 * 512], BF16))
                       for i in range(2)]
                h2b = e3(nc.sbuf_tensor("h2b", [128, E], BF16))
                ost = [e3(nc.sbuf_tensor(f"ost{i}", [128, 512], F32))
                       for i in range(2)]

                blk2 = e3(nc.Block())

                @blk2.sync
                def _(sync):
                    for lt in range(NT_LOC):
                        inc(sync.dma_start(
                            out=x1[:, lt * E:(lt + 1) * E],
                            in_=bass.AP(x_q, lt * 128 * E, [[E, 128], [1, E]])),
                            "smisc", 16)
                    inc(sync.dma_start(
                        out=bias_bc[:, :],
                        in_=bass.AP(fbias, 0, [[0, 128], [1, E]])),
                        "smisc", 16)
                x1_ld = val("smisc")

                with ExitStack() as tps:
                    tpps3 = [tps.enter_context(
                        nc.psum_tensor(f"tpb{i}", [128, 512], F32))
                        for i in range(8)]
                    ev_hist = {}
                    for k in range(16):
                        lt, hchunk = k // 2, k % 2
                        quad = tpps3[0:4] if k % 2 == 0 else tpps3[4:8]

                        @blk2.tensor
                        def _(t, lt=lt, hchunk=hchunk, k=k, quad=quad):
                            if k >= 2:
                                t.wait_ge(sems["sv"], ev_hist[k - 2])
                            last = None
                            for c8 in range(8):
                                ch = hchunk * 8 + c8
                                last = t.matmul(
                                    quad[c8 % 4][:, (c8 // 4) * 128:
                                                 (c8 // 4 + 1) * 128],
                                    o_sb[:, lt * E + ch * 128:
                                         lt * E + (ch + 1) * 128],
                                    id16sb[:, :], start=True, stop=True)
                            inc(last, "st")
                        tpd = val("st")

                        @blk2.vector
                        def _(v, lt=lt, hchunk=hchunk, tpd=tpd, k=k, quad=quad):
                            v.wait_ge(sems["st"], tpd)
                            ins = None
                            for par in range(4):
                                ins = v.tensor_copy(
                                    bass.AP(oT,
                                            (hchunk * 8 + par) * 1024 + lt * 128,
                                            [[NCH * TLOC, 128], [4096, 2],
                                             [1, 128]]),
                                    quad[par][:, 0:256])
                            ev_hist[k] = inc(ins, "sv")
                    oT_done = val("sv")

                ost_done = {}

                def emit_mm_phase(accs, lhsT_buf, wsrc, out_to_x1,
                                  lhs_ready_sv, w_gate_scc, w_prev_use,
                                  w_gate_se=0):
                    wdone = {}
                    wuse = {}
                    ds_evac = {}

                    for ds in range(4):
                        @blk2.gpsimd
                        def _(g, ds=ds):
                            if ds == 0 and w_gate_scc:
                                g.wait_ge(sems["scc"], w_gate_scc)
                            if ds == 0 and w_gate_se:
                                g.wait_ge(sems["se"], w_gate_se)
                            if ds < 2 and w_prev_use:
                                g.wait_ge(sems["st"], w_prev_use[ds])
                            if ds >= 2:
                                g.wait_ge(sems["st"], wuse[ds - 2])
                            for din in range(16):
                                inc(g.dma_start(
                                    out=wsl[ds % 2][:, din * 512:(din + 1) * 512],
                                    in_=bass.AP(wsrc, din * 128 * E + ds * 512,
                                                [[E, 128], [1, 512]])),
                                    f"sw{ds % 2}", 16)
                        wdone[ds] = (f"sw{ds % 2}", val(f"sw{ds % 2}"))

                        @blk2.tensor
                        def _(t, ds=ds):
                            t.wait_ge(sems["sv"], lhs_ready_sv)
                            if ds >= 1:
                                t.wait_ge(sems["sv"], ds_evac[ds - 1])
                            t.wait_ge(sems[wdone[ds][0]], wdone[ds][1])
                            for din in range(16):
                                for lt in range(NT_LOC):
                                    acc = accs[lt // 4]
                                    i = lt % 4
                                    last = t.matmul(
                                        acc[:, i * 512:(i + 1) * 512],
                                        lhsT_buf[:, din * 1024 + lt * 128:
                                                 din * 1024 + (lt + 1) * 128],
                                        wsl[ds % 2][:, din * 512:(din + 1) * 512],
                                        start=(din == 0), stop=(din == 15))
                            wuse[ds] = inc(last, "st")

                        @blk2.vector
                        def _(v, ds=ds):
                            v.wait_ge(sems["st"], wuse[ds])
                            v.wait_ge(sems["smisc"], x1_ld)
                            last = None
                            for lt in range(NT_LOC):
                                acc = accs[lt // 4]
                                i = lt % 4
                                xsl = x1[:, lt * E + ds * 512:
                                         lt * E + (ds + 1) * 512]
                                if out_to_x1:
                                    last = v.scalar_tensor_tensor(
                                        xsl, acc[:, i * 512:(i + 1) * 512],
                                        1.0, xsl, op0=ALU.mult, op1=ALU.add)
                                else:
                                    idx = ds * NT_LOC + lt
                                    if idx >= 2:
                                        v.wait_ge(sems[f"so{idx % 2}"],
                                                  16 * (idx // 2))
                                    ost_done[(ds, lt)] = inc(
                                        v.scalar_tensor_tensor(
                                            ost[idx % 2][:, :],
                                            acc[:, i * 512:(i + 1) * 512],
                                            1.0, xsl,
                                            op0=ALU.mult, op1=ALU.add), "sv")
                            if out_to_x1:
                                ds_evac[ds] = inc(last, "sv")
                            else:
                                ds_evac[ds] = val("sv")

                        if not out_to_x1:
                            @blk2.sync
                            def _(sync, ds=ds):
                                for lt in range(NT_LOC):
                                    idx = ds * NT_LOC + lt
                                    sync.wait_ge(sems["sv"],
                                                 ost_done[(ds, lt)])
                                    got = inc(sync.dma_start(
                                        out=bass.AP(out,
                                                    lt * 128 * E + ds * 512,
                                                    [[E, 128], [1, 512]]),
                                        in_=ost[idx % 2][:, :]),
                                        f"so{idx % 2}", 16)
                                    assert got == 16 * (idx // 2 + 1)
                    return val("sv"), wuse

                with ExitStack() as accsc:
                    pa = accsc.enter_context(
                        nc.psum_tensor("pa3", [128, 2048], F32))
                    pb = accsc.enter_context(
                        nc.psum_tensor("pb3", [128, 2048], F32))
                    wo_done, wo_use = emit_mm_phase(
                        (pa, pb), oT, Wo, True, oT_done, 0, None)

                ln2_last = 0
                ln2_sq_last = [0]
                for lt in range(NT_LOC):
                    @blk2.vector
                    def _(v, lt=lt):
                        if lt == 0:
                            v.wait_ge(sems["sv"], wo_done)
                            v.wait_ge(sems["smisc"], x1_ld)
                        xsl = x1[:, lt * E:(lt + 1) * E]
                        rg = inc(v.reduce_sum(st_m[:, :], xsl, axis=AX,
                                              opt_output=False), "sv")
                        v.wait_ge(sems["sv"], rg)
                        inc(v.tensor_scalar_mul(st_m[:, :], st_m[:, :],
                                                1.0 / E), "sv")
                    msum = val("sv")

                    @blk2.scalar
                    def _(s, lt=lt, msum=msum):
                        s.wait_ge(sems["sv"], msum)
                        inc(s.activation(wsl[0][:, 0:E],
                                         x1[:, lt * E:(lt + 1) * E],
                                         AF.Square, accum_out=st_ss[:, :]),
                            "se")
                    ssum = val("se")
                    ln2_sq_last[0] = ssum

                    @blk2.vector
                    def _(v, lt=lt, ssum=ssum):
                        v.wait_ge(sems["se"], ssum)
                        v.tensor_scalar(st_v1[:, :], st_ss[:, :], 1.0 / E,
                                        EPS, op0=ALU.mult, op1=ALU.add)
                        vw = inc(v.tensor_tensor(st_v2[:, :], st_m[:, :],
                                                 st_m[:, :], op=ALU.mult),
                                 "sv")
                        v.wait_ge(sems["sv"], vw)
                        inc(v.tensor_tensor(st_v1[:, :], st_v1[:, :],
                                            st_v2[:, :], op=ALU.subtract),
                            "sv")
                    vdone = val("sv")

                    @blk2.scalar
                    def _(s, lt=lt, vdone=vdone):
                        s.wait_ge(sems["sv"], vdone)
                        lw = inc(s.activation(st_r[:, :], st_v1[:, :], AF.Ln,
                                              bias=0.0, scale=8.0), "se")
                        s.wait_ge(sems["se"], lw)
                        inc(s.activation(st_r[:, :], st_r[:, :], AF.Exp,
                                         bias=0.0, scale=-0.5), "se")
                    rdone = val("se")

                    @blk2.vector
                    def _(v, lt=lt, rdone=rdone):
                        v.wait_ge(sems["se"], rdone)
                        xsl = x1[:, lt * E:(lt + 1) * E]
                        v.tensor_scalar(h2b[:, :], xsl, st_m[:, :],
                                        st_r[:, :], op0=ALU.subtract,
                                        op1=ALU.mult)
                        inc(v.tensor_tensor(xsl, xsl, bias_bc[:, :],
                                            op=ALU.add), "sv")
                    h2_done = val("sv")

                    with ExitStack() as tp2:
                        tpps4 = [tp2.enter_context(
                            nc.psum_tensor(f"tpc{lt}_{i}", [128, 512], F32))
                            for i in range(8)]
                        for hchunk in range(2):
                            quad = tpps4[0:4] if hchunk == 0 else tpps4[4:8]

                            @blk2.tensor
                            def _(t, lt=lt, hchunk=hchunk, h2_done=h2_done,
                                  quad=quad):
                                t.wait_ge(sems["sv"], h2_done)
                                last = None
                                for c8 in range(8):
                                    ch = hchunk * 8 + c8
                                    last = t.matmul(
                                        quad[c8 % 4][:, (c8 // 4) * 128:
                                                     (c8 // 4 + 1) * 128],
                                        h2b[:, ch * 128:(ch + 1) * 128],
                                        id16sb[:, :], start=True, stop=True)
                                inc(last, "st")
                            tpd = val("st")

                            @blk2.vector
                            def _(v, lt=lt, hchunk=hchunk, tpd=tpd, quad=quad):
                                v.wait_ge(sems["st"], tpd)
                                ins = None
                                for par in range(4):
                                    ins = v.tensor_copy(
                                        bass.AP(oT,
                                                (hchunk * 8 + par) * 1024 + lt * 128,
                                                [[NCH * TLOC, 128], [4096, 2],
                                                 [1, 128]]),
                                        quad[par][:, 0:256])
                                inc(ins, "sv")
                    ln2_last = val("sv")

                with ExitStack() as accsc2:
                    pa = accsc2.enter_context(
                        nc.psum_tensor("pa4", [128, 2048], F32))
                    pb = accsc2.enter_context(
                        nc.psum_tensor("pb4", [128, 2048], F32))
                    emit_mm_phase((pa, pb), oT, ar_out, False, ln2_last,
                                  AR_DONE, [wo_use[2], wo_use[3]],
                                  w_gate_se=ln2_sq_last[0])

                @blk2.sync
                def _(sync):
                    sync.wait_ge(sems["so0"], val("so0"))
                    sync.wait_ge(sems["so1"], val("so1"))

    return nc


_CACHE = {}


def _get_program():
    if "nc" not in _CACHE:
        _CACHE["nc"] = build_program()
    return _CACHE["nc"]


def _np32(a):
    return np.asarray(a).astype(np.float32, copy=False)


def kernel(L0_attn_x, L0_attn_Wq, L0_attn_Wk, L0_attn_Wv, L0_attn_Wo,
           L0_attn_lambda_q1, L0_attn_lambda_k1, L0_attn_lambda_q2,
           L0_attn_lambda_k2, L0_attn_g, L0_ffn_factors, L0_ffn_bias):
    x = np.ascontiguousarray(_np32(L0_attn_x)[0])
    Wq_ = np.ascontiguousarray(_np32(L0_attn_Wq))
    Wk_ = np.ascontiguousarray(_np32(L0_attn_Wk))
    Wv_ = np.ascontiguousarray(_np32(L0_attn_Wv))
    Wo_ = np.ascontiguousarray(_np32(L0_attn_Wo))
    fct = np.ascontiguousarray(_np32(L0_ffn_factors))
    fb = np.ascontiguousarray(_np32(L0_ffn_bias))[None, :]
    lam = np.ascontiguousarray(np.stack([
        _np32(L0_attn_lambda_q1), _np32(L0_attn_lambda_k1),
        _np32(L0_attn_lambda_q2), _np32(L0_attn_lambda_k2)]))
    g = np.ascontiguousarray(_np32(L0_attn_g))[None, :]
    id16m = np.eye(128, dtype=ml_dtypes.bfloat16)
    tri = np.triu(np.ones((128, 128), dtype=np.float32))

    nc = _get_program()
    in_maps = []
    for r in range(NC):
        qtiles = [8 * s + r for s in range(8)]
        x_q = np.concatenate([x[t * 128:(t + 1) * 128] for t in qtiles], axis=0)
        x_kv = x[r * 1024:(r + 1) * 1024]
        bm = np.zeros((8, 128, 128), dtype=np.float32)
        for jp in range(8):
            if jp < r:
                bm[jp] = 1.0
            elif jp == r:
                bm[jp] = tri
        in_maps.append({
            "x_q": np.ascontiguousarray(x_q),
            "x_kv": np.ascontiguousarray(x_kv),
            "Wq": Wq_, "Wk": Wk_, "Wv": Wv_, "Wo": Wo_,
            "fac": np.ascontiguousarray(fct[r]),
            "fbias": fb, "lamv": lam, "gvec": g,
            "id16": id16m,
            "bmask": np.ascontiguousarray(
                bm.reshape(8 * 128, 128).astype(ml_dtypes.bfloat16)),
        })

    res = run_bass_kernel_spmd(nc, in_maps, core_ids=list(range(NC)))

    outp = np.zeros((T, E), dtype=np.float32)
    for r in range(NC):
        o = np.asarray(res.results[r]["out"])
        for s in range(8):
            t = 8 * s + r
            outp[t * 128:(t + 1) * 128] = o[s * 128:(s + 1) * 128]
    return outp[None, :, :]


# version 12 (final)
# speedup vs baseline: 1.1240x; 1.1240x over previous
import math
from contextlib import ExitStack

import numpy as np
import ml_dtypes

import concourse.bass as bass
import concourse.mybir as mybir
from concourse.bass_utils import run_bass_kernel_spmd

F32 = mybir.dt.float32
BF16 = mybir.dt.bfloat16
AF = mybir.ActivationFunctionType
ALU = mybir.AluOpType
AX = mybir.AxisListType.X

NC = 8
T = 8192
E = 2048
NCH = 16
TLOC = 1024
NT_LOC = 8
LAMBDA_INIT = 0.8 - 0.6 * math.exp(-0.3 * 12)
EPS = 1e-5
SCALE = 128.0 ** -0.5

KOFF = 0
KSZ = NCH * 128 * TLOC
VOFF = KSZ
VROW = 8 * 258
TILE_V = 128 * VROW
VSZ = NT_LOC * TILE_V
SOFF = VOFF + VSZ
RSZ = SOFF + 16
KSZ2 = KSZ + 16


def band_r0(grp, j):
    return min(3, max(0, j // 8 - 4 * grp))


def slot_last_j(grp, r):
    return 8 * (4 * grp + r) + 7


def build_program(detect_races=False):
    nc = bass.Bass("TRN2", target_bir_lowering=False, debug=False,
                   detect_race_conditions=detect_races)

    x_q = nc.dram_tensor("x_q", [TLOC, E], F32, kind="ExternalInput")
    x_kv = nc.dram_tensor("x_kv", [TLOC, E], F32, kind="ExternalInput")
    Wq = nc.dram_tensor("Wq", [E, E], F32, kind="ExternalInput")
    Wk = nc.dram_tensor("Wk", [E, E], F32, kind="ExternalInput")
    Wv = nc.dram_tensor("Wv", [E, E], F32, kind="ExternalInput")
    Wo = nc.dram_tensor("Wo", [E, E], F32, kind="ExternalInput")
    fac = nc.dram_tensor("fac", [E, E], F32, kind="ExternalInput")
    fbias = nc.dram_tensor("fbias", [1, E], F32, kind="ExternalInput")
    lamv = nc.dram_tensor("lamv", [4, 128], F32, kind="ExternalInput")
    gvec = nc.dram_tensor("gvec", [1, 256], F32, kind="ExternalInput")
    id16 = nc.dram_tensor("id16", [128, 128], BF16, kind="ExternalInput")
    bmask = nc.dram_tensor("bmask", [8 * 128, 128], BF16, kind="ExternalInput")

    out = nc.dram_tensor("out", [TLOC, E], F32, kind="ExternalOutput")

    agk_in = nc.dram_tensor("agk_in", [KSZ2], BF16)
    agk_out = nc.dram_tensor("agk_out", [NC * KSZ2], BF16,
                             addr_space="Shared")
    agv_in = nc.dram_tensor("agv_in", [VSZ], BF16)
    agv_out = nc.dram_tensor("agv_out", [NC * VSZ], BF16,
                             addr_space="Shared")
    ar_in = nc.dram_tensor("ar_in", [E, E], F32)
    ar_out = nc.dram_tensor("ar_out", [E, E], F32, addr_space="Shared")
    bscr = nc.dram_tensor("bscr", [128], F32)

    with ExitStack() as top:
        en = top.enter_context

        sems = {}
        cnt = {}
        for name in ("st", "se", "sv", "scc", "smisc", "sar", "sgp",
                     "sx0", "sx1", "sw0", "sw1", "svs0", "svs1",
                     "sk0", "sk1", "svh0", "svh1", "so0", "so1"):
            sems[name] = en(nc.semaphore(name))
            cnt[name] = 0

        def inc(instr, which, n=1):
            instr.then_inc(sems[which], n)
            cnt[which] += n
            return cnt[which]

        def val(which):
            return cnt[which]

        id16sb = en(nc.sbuf_tensor("id16sb", [128, 128], BF16))
        masks = en(nc.sbuf_tensor("masks", [128, 8 * 128], BF16))
        ones_c = en(nc.sbuf_tensor("ones_c", [128, 1], BF16))
        ones_r32 = en(nc.sbuf_tensor("ones_r32", [1, 128], F32))
        g_bc = en(nc.sbuf_tensor("g_bc", [128, 256], F32))
        lam_bc = en(nc.sbuf_tensor("lam_bc", [128, 1], F32))
        negM = en(nc.sbuf_tensor("negM", [128, 32], F32))
        qm2 = en(nc.sbuf_tensor("qm2", [1, 32], F32))
        kg = en(nc.sbuf_tensor("kg", [1, 16], F32))
        kmx_all = en(nc.sbuf_tensor("kmx_all", [1, 128], BF16))
        smalls = en(nc.sbuf_tensor("smalls", [1, 160], F32))
        st_m = en(nc.sbuf_tensor("st_m", [128, 1], F32))
        st_ss = en(nc.sbuf_tensor("st_ss", [128, 1], F32))
        st_v1 = en(nc.sbuf_tensor("st_v1", [128, 1], F32))
        st_v2 = en(nc.sbuf_tensor("st_v2", [128, 1], F32))
        st_r = en(nc.sbuf_tensor("st_r", [128, 1], F32))
        fin_sc = en(nc.sbuf_tensor("fin_sc", [128, 8], F32))
        grow = en(nc.sbuf_tensor("grow", [1, 256], F32))
        lrow = en(nc.sbuf_tensor("lrow", [1, 512], F32))
        o_sb = en(nc.sbuf_tensor("o_sb", [128, NT_LOC * E], BF16))

        qn_done_ref = [0]
        AGK_DONE, AGV_DONE, AR_DONE = 1, 2, 3

        def emit_ln_tiles(blk, src_dram, xb, hb, dump, dst_hT, tpps,
                          gate_st, gate_sv, id_gate):
            apply_done = [0] * NT_LOC
            x_done = [0] * NT_LOC
            tp_done = {}
            ev_done = {}
            for i in range(NT_LOC):
                b = i % 2

                @blk.sync
                def _(sync, i=i, b=b):
                    if i >= 2:
                        sync.wait_ge(sems["sv"], apply_done[i - 2])
                    elif gate_sv:
                        sync.wait_ge(sems["sv"], gate_sv)
                    x_done[i] = inc(sync.dma_start(
                        out=xb[b][:, :],
                        in_=bass.AP(src_dram, i * 128 * E, [[E, 128], [1, E]])),
                        f"sx{b}", 16)

                @blk.vector
                def _(v, i=i, b=b):
                    v.wait_ge(sems[f"sx{b}"], x_done[i])
                    rg = inc(v.reduce_sum(st_m[:, :], xb[b][:, :], axis=AX,
                                          opt_output=False), "sv")
                    v.wait_ge(sems["sv"], rg)
                    inc(v.tensor_scalar_mul(st_m[:, :], st_m[:, :], 1.0 / E), "sv")
                m_done = val("sv")

                @blk.scalar
                def _(s, i=i, b=b):
                    s.wait_ge(sems[f"sx{b}"], x_done[i])
                    inc(s.activation(dump[:, :], xb[b][:, :], AF.Square,
                                     accum_out=st_ss[:, :]), "se")
                ss_done = val("se")

                @blk.vector
                def _(v, i=i, m_done=m_done):
                    v.wait_ge(sems["se"], ss_done)
                    v.wait_ge(sems["sv"], m_done)
                    v.tensor_scalar(st_v1[:, :], st_ss[:, :], 1.0 / E, EPS,
                                    op0=ALU.mult, op1=ALU.add)
                    vw = inc(v.tensor_tensor(st_v2[:, :], st_m[:, :],
                                             st_m[:, :], op=ALU.mult), "sv")
                    v.wait_ge(sems["sv"], vw)
                    inc(v.tensor_tensor(st_v1[:, :], st_v1[:, :], st_v2[:, :],
                                        op=ALU.subtract), "sv")
                var_done = val("sv")

                @blk.scalar
                def _(s, i=i):
                    s.wait_ge(sems["sv"], var_done)
                    lw = inc(s.activation(st_r[:, :], st_v1[:, :], AF.Ln,
                                          bias=0.0, scale=1.0), "se")
                    s.wait_ge(sems["se"], lw)
                    inc(s.activation(st_r[:, :], st_r[:, :], AF.Exp,
                                     bias=0.0, scale=-0.5), "se")
                r_done = val("se")

                @blk.vector
                def _(v, i=i, b=b, m_done=m_done):
                    v.wait_ge(sems["se"], r_done)
                    v.wait_ge(sems["sv"], m_done)
                    if i >= 1:
                        v.wait_ge(sems["st"], tp_done[(i - 1, 1)])
                    apply_done[i] = inc(v.tensor_scalar(
                        hb[b][:, :], xb[b][:, :], st_m[:, :], st_r[:, :],
                        op0=ALU.subtract, op1=ALU.mult), "sv")

                for hchunk in range(2):
                    k = i * 2 + hchunk
                    quad = tpps[0:4] if k % 2 == 0 else tpps[4:8]

                    @blk.tensor
                    def _(t, i=i, b=b, hchunk=hchunk, k=k, quad=quad):
                        t.wait_ge(sems["sv"], apply_done[i])
                        if k >= 2:
                            t.wait_ge(sems["sv"], ev_done[k - 2])
                        else:
                            t.wait_ge(sems["smisc"], id_gate)
                            if gate_st:
                                t.wait_ge(sems["st"], gate_st)
                            if gate_sv:
                                t.wait_ge(sems["sv"], gate_sv)
                        last = None
                        for c8 in range(8):
                            ch = hchunk * 8 + c8
                            last = t.matmul(
                                quad[c8 % 4][:, (c8 // 4) * 128:(c8 // 4 + 1) * 128],
                                hb[b][:, ch * 128:(ch + 1) * 128],
                                id16sb[:, :], start=True, stop=True)
                        tp_done[(i, hchunk)] = inc(last, "st")

                    @blk.vector
                    def _(v, i=i, hchunk=hchunk, k=k, quad=quad):
                        v.wait_ge(sems["st"], tp_done[(i, hchunk)])
                        ins = None
                        for par in range(4):
                            ins = v.tensor_copy(
                                bass.AP(dst_hT,
                                        (hchunk * 8 + par) * 1024 + i * 128,
                                        [[NCH * TLOC, 128], [4096, 2], [1, 128]]),
                                quad[par][:, 0:256])
                        ev_done[k] = inc(ins, "sv")
            return ev_done[2 * NT_LOC - 1]

        p12 = ExitStack()
        if True:
            qT = p12.enter_context(nc.sbuf_tensor("qT", [128, NCH * TLOC], BF16))

            with ExitStack() as p1:
                e1 = p1.enter_context
                xb = [e1(nc.sbuf_tensor(f"xb{i}", [128, E], F32)) for i in range(2)]
                hb = [e1(nc.sbuf_tensor(f"hb{i}", [128, E], BF16)) for i in range(2)]
                dump = e1(nc.sbuf_tensor("dump", [128, E], BF16))
                hT = e1(nc.sbuf_tensor("hT", [128, NCH * TLOC], BF16))
                wfh = [e1(nc.sbuf_tensor(f"wfh{i}", [128, 16 * 512], BF16))
                       for i in range(2)]
                kT_own = e1(nc.sbuf_tensor("kT_own", [128, NCH * TLOC], BF16))
                vstg = [e1(nc.sbuf_tensor(f"vstg{i}", [128, 4 * 2 * 258], BF16))
                        for i in range(2)]
                sqb = e1(nc.sbuf_tensor("sqb", [128, 1024], BF16))
                kmx_own = e1(nc.sbuf_tensor("kmx_own", [1, 16], F32))
                kmx_own16 = e1(nc.sbuf_tensor("kmx_own16", [1, 16], BF16))

                blk0 = e1(nc.Block())

                @blk0.gpsimd
                def _(g):
                    g.memset(ones_c[:, :], 1.0)
                    g.memset(ones_r32[:1, :], 1.0)
                    last = None
                    for vb in range(2):
                        g.memset(bass.AP(vstg[vb], 256,
                                         [[4 * 2 * 258, 128], [258, 8], [1, 1]]),
                                 1.0)
                        last = g.memset(bass.AP(vstg[vb], 257,
                                                [[4 * 2 * 258, 128], [258, 8],
                                                 [1, 1]]),
                                        0.0)
                    inc(last, "sgp")
                    inc(g.dma_start(out=ar_in.ap(), in_=fac.ap()), "sar", 16)
                memset_done = 1
                arb_done = val("sar")

                @blk0.sync
                def _(sync):
                    inc(sync.dma_start(out=id16sb[:, :], in_=id16.ap()), "smisc", 16)
                    inc(sync.dma_start(
                        out=masks[:, :],
                        in_=bass.AP(bmask, 0, [[128, 128], [128 * 128, 8], [1, 128]])),
                        "smisc", 16)
                    inc(sync.dma_start(out=grow[:1, :], in_=gvec.ap()), "smisc", 16)
                    for rr in range(4):
                        inc(sync.dma_start(
                            out=lrow[0:1, rr * 128:(rr + 1) * 128],
                            in_=bass.AP(lamv, rr * 128, [[128, 1], [1, 128]])),
                            "smisc", 16)
                const_ld = val("smisc")

                id_gate = const_ld

                @blk0.scalar
                def _(s):
                    s.wait_ge(sems["sgp"], memset_done)
                    s.activation(st_v2[:, :], ones_c[:, :], AF.Square,
                                 accum_out=st_v1[:, :])
                    s.activation(st_v2[:, :], ones_c[:, :], AF.Ln,
                                 bias=0.0, scale=1.0)
                    s.activation(st_v2[:, :], ones_c[:, :], AF.Exp,
                                 bias=0.0, scale=1.0)

                @blk0.vector
                def _(v):
                    v.wait_ge(sems["sgp"], memset_done)
                    v.reciprocal(st_v2[:, :], ones_c[:, :])

                with ExitStack() as tpsc:
                    tpps = [tpsc.enter_context(
                        nc.psum_tensor(f"tpA{i}", [128, 512], F32))
                        for i in range(8)]
                    kv_hT_done = emit_ln_tiles(blk0, x_kv, xb, hb, dump, hT,
                                               tpps, 0, 0, id_gate)

                def w_qtr_loads(wsrc, qtr, wait_st=0):
                    sem = f"sw{qtr % 2}"

                    @blk0.gpsimd
                    def _(g):
                        if wait_st:
                            g.wait_ge(sems["st"], wait_st)
                        for din in range(16):
                            inc(g.dma_start(
                                out=wfh[qtr % 2][:, din * 512:(din + 1) * 512],
                                in_=bass.AP(wsrc, din * 128 * E + qtr * 512,
                                            [[E, 128], [1, 512]])), sem, 16)
                    return (sem, val(sem))

                with ExitStack() as accsc:
                    pacc0 = accsc.enter_context(
                        nc.psum_tensor("pacc0", [128, 2048], F32))
                    pacc1 = accsc.enter_context(
                        nc.psum_tensor("pacc1", [128, 2048], F32))

                    def proj_fm(hT_ready_sv, wsrc, w_first_gate, dst,
                                dst_scale):
                        evac_done = {}
                        qtr_use = {}
                        rot = 0
                        for qtr in range(4):
                            gate = qtr_use[qtr - 2] if qtr >= 2 else w_first_gate
                            wdone = w_qtr_loads(wsrc, qtr, wait_st=gate)
                            for grp in range(2):
                                accs = pacc0 if rot % 2 == 0 else pacc1
                                myrot = rot
                                rot += 1

                                @blk0.tensor
                                def _(t, qtr=qtr, grp=grp, accs=accs,
                                      myrot=myrot, wdone=wdone):
                                    t.wait_ge(sems["sv"], hT_ready_sv)
                                    if myrot >= 2:
                                        t.wait_ge(sems["sv"],
                                                  evac_done[myrot - 2])
                                    t.wait_ge(sems[wdone[0]], wdone[1])
                                    last = None
                                    for din in range(16):
                                        for i in range(4):
                                            last = t.matmul(
                                                accs[:, i * 512:(i + 1) * 512],
                                                wfh[qtr % 2][:, din * 512 + i * 128:
                                                             din * 512 + (i + 1) * 128],
                                                hT[:, din * 1024 + grp * 512:
                                                   din * 1024 + (grp + 1) * 512],
                                                start=(din == 0),
                                                stop=(din == 15))
                                    inc(last, "st")
                                mm_done = val("st")
                                qtr_use[qtr] = mm_done

                                @blk0.vector
                                def _(v, qtr=qtr, grp=grp, accs=accs,
                                      myrot=myrot, mm_done=mm_done):
                                    v.wait_ge(sems["st"], mm_done)
                                    ins = None
                                    for i in range(4):
                                        dout = qtr * 4 + i
                                        col = dout * 1024 + grp * 512
                                        if dst_scale is None:
                                            ins = v.tensor_copy(
                                                dst[:, col:col + 512],
                                                accs[:, i * 512:(i + 1) * 512])
                                        else:
                                            ins = v.tensor_scalar_mul(
                                                dst[:, col:col + 512],
                                                accs[:, i * 512:(i + 1) * 512],
                                                dst_scale)
                                    evac_done[myrot] = inc(ins, "sv")
                        return val("sv"), qtr_use[3]

                    def emit_norms(src, ready_sv, put):
                        prev_t = 0
                        prev_v = 0
                        for hc in range(16):
                            @blk0.scalar
                            def _(s, hc=hc, prev_t=prev_t):
                                if hc == 0:
                                    s.wait_ge(sems["sv"], ready_sv)
                                else:
                                    s.wait_ge(sems["st"], prev_t)
                                inc(s.activation(
                                    sqb[:, :], src[:, hc * 1024:(hc + 1) * 1024],
                                    AF.Square), "se")
                            sq_done = val("se")

                            @blk0.tensor
                            def _(t, hc=hc, sq_done=sq_done, prev_v=prev_v):
                                t.wait_ge(sems["se"], sq_done)
                                if prev_v:
                                    t.wait_ge(sems["sv"], prev_v)
                                t.matmul(pacc0[0:1, 0:512], ones_c[:, :],
                                         sqb[:, 0:512], start=True, stop=True)
                                inc(t.matmul(pacc0[0:1, 512:1024], ones_c[:, :],
                                             sqb[:, 512:1024], start=True,
                                             stop=True), "st")
                            prev_t = val("st")

                            @blk0.vector
                            def _(v, hc=hc, prev_t=prev_t):
                                v.wait_ge(sems["st"], prev_t)
                                inc(put(v, hc), "sv")
                            prev_v = val("sv")
                        return val("sv")

                    k_done, k_wlast = proj_fm(kv_hT_done, Wk, 0, kT_own, None)

                    def put_k(v, hc):
                        return v.reduce_max(kmx_own[:1, hc:hc + 1],
                                            pacc0[0:1, 0:1024], axis=AX, opt_output=False)
                    knorm_done = emit_norms(kT_own, k_done, put_k)

                    @blk0.vector
                    def _(v):
                        v.wait_ge(sems["sv"], knorm_done)
                        inc(v.tensor_copy(kmx_own16[:1, :], kmx_own[:1, :]), "sv")
                    kmx16_done = val("sv")

                    @blk0.sync
                    def _(sync):
                        sync.wait_ge(sems["sv"], k_done)
                        for q in range(4):
                            inc(sync.dma_start(
                                out=bass.AP(agk_in, KOFF + q * 4 * 128 * 1024,
                                            [[1024, 128], [128 * 1024, 4],
                                             [1, 1024]]),
                                in_=bass.AP(kT_own, q * 4 * 1024,
                                            [[NCH * TLOC, 128], [1024, 4],
                                             [1, 1024]])), "smisc", 16)
                        sync.wait_ge(sems["sv"], kmx16_done)
                        inc(sync.dma_start(
                            out=bass.AP(agk_in, KSZ, [[16, 1], [1, 16]]),
                            in_=kmx_own16[:1, :]), "smisc", 16)
                    kstage_done = val("smisc")

                    v_evac = {}
                    v_dma = {}
                    v_qtr_use = {}
                    vrot = 0
                    for qtr in range(4):
                        gate = v_qtr_use[qtr - 2] if qtr >= 2 else k_wlast
                        wv_done = w_qtr_loads(Wv, qtr, wait_st=gate)
                        for blki in range(2):
                            accs = pacc0 if vrot % 2 == 0 else pacc1
                            myrot = vrot
                            vrot += 1

                            @blk0.tensor
                            def _(t, qtr=qtr, blki=blki, accs=accs,
                                  myrot=myrot, wv_done=wv_done):
                                t.wait_ge(sems["sv"], kv_hT_done)
                                if myrot == 0:
                                    t.wait_ge(sems["sv"], knorm_done)
                                if myrot >= 2:
                                    t.wait_ge(sems["sv"], v_evac[myrot - 2])
                                t.wait_ge(sems[wv_done[0]], wv_done[1])
                                last = None
                                for din in range(16):
                                    for i in range(4):
                                        tile = blki * 4 + i
                                        last = t.matmul(
                                            accs[:, i * 512:(i + 1) * 512],
                                            hT[:, din * 1024 + tile * 128:
                                               din * 1024 + (tile + 1) * 128],
                                            wfh[qtr % 2][:, din * 512:(din + 1) * 512],
                                            start=(din == 0), stop=(din == 15))
                                inc(last, "st")
                            vm_done = val("st")
                            v_qtr_use[qtr] = vm_done

                            @blk0.vector
                            def _(v, accs=accs, myrot=myrot, vm_done=vm_done):
                                v.wait_ge(sems["st"], vm_done)
                                v.wait_ge(sems["sgp"], memset_done)
                                if myrot >= 2:
                                    v.wait_ge(sems[f"svs{myrot % 2}"],
                                              v_dma[myrot - 2])
                                inc(v.tensor_copy(
                                    bass.AP(vstg[myrot % 2], 0,
                                            [[4 * 2 * 258, 128], [516, 4],
                                             [258, 2], [1, 256]]),
                                    bass.AP(accs, 0,
                                            [[2048, 128], [512, 4],
                                             [256, 2], [1, 256]])), "sv")
                            v_evac[myrot] = val("sv")

                            @blk0.sync
                            def _(sync, qtr=qtr, blki=blki, myrot=myrot):
                                sync.wait_ge(sems["sv"], v_evac[myrot])
                                h0 = qtr * 2
                                v_dma[myrot] = inc(sync.dma_start(
                                    out=bass.AP(agv_in,
                                                blki * 4 * TILE_V + h0 * 258,
                                                [[VROW, 128], [TILE_V, 4],
                                                 [258, 2], [1, 258]]),
                                    in_=bass.AP(vstg[myrot % 2], 0,
                                                [[4 * 2 * 258, 128], [516, 4],
                                                 [258, 2], [1, 258]])),
                                    f"svs{myrot % 2}", 16)
                    vstage0 = val("svs0")
                    vstage1 = val("svs1")

                    @blk0.gpsimd
                    def _(g):
                        g.wait_ge(sems["smisc"], kstage_done)
                        inc(g.collective_compute(
                            "AllGather", ALU.bypass,
                            replica_groups=[list(range(NC))],
                            ins=[agk_in.ap()], outs=[agk_out.ap()]), "scc")
                        g.wait_ge(sems["svs0"], vstage0)
                        g.wait_ge(sems["svs1"], vstage1)
                        inc(g.collective_compute(
                            "AllGather", ALU.bypass,
                            replica_groups=[list(range(NC))],
                            ins=[agv_in.ap()], outs=[agv_out.ap()]), "scc")
                        g.wait_ge(sems["sar"], arb_done)
                        inc(g.collective_compute(
                            "AllReduce", ALU.add,
                            replica_groups=[list(range(NC))],
                            ins=[ar_in.ap()], outs=[ar_out.ap()]), "scc")

                    q_gate_st = val("st")
                    q_gate_sv = val("sv")

                with ExitStack() as tpsc2:
                    tpps2 = [tpsc2.enter_context(
                        nc.psum_tensor(f"tpB{i}", [128, 512], F32))
                        for i in range(8)]
                    q_hT_done = emit_ln_tiles(blk0, x_q, xb, hb, dump, hT, tpps2,
                                              q_gate_st, q_gate_sv, id_gate)

                with ExitStack() as accsc2:
                    pacc0 = accsc2.enter_context(
                        nc.psum_tensor("pacc0b", [128, 2048], F32))
                    pacc1 = accsc2.enter_context(
                        nc.psum_tensor("pacc1b", [128, 2048], F32))

                    def proj_fm2(hT_ready_sv, wsrc, w_first_gate, dst,
                                 dst_scale):
                        evac_done = {}
                        qtr_use = {}
                        rot = 0
                        for qtr in range(4):
                            gate = qtr_use[qtr - 2] if qtr >= 2 else w_first_gate
                            wdone = w_qtr_loads(wsrc, qtr, wait_st=gate)
                            for grp in range(2):
                                accs = pacc0 if rot % 2 == 0 else pacc1
                                myrot = rot
                                rot += 1

                                @blk0.tensor
                                def _(t, qtr=qtr, grp=grp, accs=accs,
                                      myrot=myrot, wdone=wdone):
                                    t.wait_ge(sems["sv"], hT_ready_sv)
                                    if myrot >= 2:
                                        t.wait_ge(sems["sv"],
                                                  evac_done[myrot - 2])
                                    t.wait_ge(sems[wdone[0]], wdone[1])
                                    last = None
                                    for din in range(16):
                                        for i in range(4):
                                            last = t.matmul(
                                                accs[:, i * 512:(i + 1) * 512],
                                                wfh[qtr % 2][:, din * 512 + i * 128:
                                                             din * 512 + (i + 1) * 128],
                                                hT[:, din * 1024 + grp * 512:
                                                   din * 1024 + (grp + 1) * 512],
                                                start=(din == 0),
                                                stop=(din == 15))
                                    inc(last, "st")
                                mm_done = val("st")
                                qtr_use[qtr] = mm_done

                                @blk0.vector
                                def _(v, qtr=qtr, grp=grp, accs=accs,
                                      myrot=myrot, mm_done=mm_done):
                                    v.wait_ge(sems["st"], mm_done)
                                    ins = None
                                    for i in range(4):
                                        dout = qtr * 4 + i
                                        col = dout * 1024 + grp * 512
                                        ins = v.tensor_scalar_mul(
                                            dst[:, col:col + 512],
                                            accs[:, i * 512:(i + 1) * 512],
                                            dst_scale)
                                    evac_done[myrot] = inc(ins, "sv")
                        return val("sv")

                    q_done = proj_fm2(q_hT_done, Wq, val("st"), qT, SCALE)

                    def emit_norms2(src, ready_sv, put):
                        prev_t = 0
                        prev_v = 0
                        for hc in range(16):
                            @blk0.scalar
                            def _(s, hc=hc, prev_t=prev_t):
                                if hc == 0:
                                    s.wait_ge(sems["sv"], ready_sv)
                                else:
                                    s.wait_ge(sems["st"], prev_t)
                                inc(s.activation(
                                    sqb[:, :], src[:, hc * 1024:(hc + 1) * 1024],
                                    AF.Square), "se")
                            sq_done = val("se")

                            @blk0.tensor
                            def _(t, hc=hc, sq_done=sq_done, prev_v=prev_v):
                                t.wait_ge(sems["se"], sq_done)
                                if prev_v:
                                    t.wait_ge(sems["sv"], prev_v)
                                t.matmul(pacc0[0:1, 0:512], ones_c[:, :],
                                         sqb[:, 0:512], start=True, stop=True)
                                inc(t.matmul(pacc0[0:1, 512:1024], ones_c[:, :],
                                             sqb[:, 512:1024], start=True,
                                             stop=True), "st")
                            prev_t = val("st")

                            @blk0.vector
                            def _(v, hc=hc, prev_t=prev_t):
                                v.wait_ge(sems["st"], prev_t)
                                inc(put(v, hc), "sv")
                            prev_v = val("sv")
                        return val("sv")

                    def put_q(v, hc):
                        v.reduce_max(qm2[:1, hc:hc + 1], pacc0[0:1, 0:512],
                                     axis=AX, opt_output=False)
                        return v.reduce_max(qm2[:1, 16 + hc:17 + hc],
                                            pacc0[0:1, 512:1024], axis=AX, opt_output=False)
                    qn_done_ref[0] = emit_norms2(qT, q_done, put_q)

                    @blk0.vector
                    def _(v):
                        v.wait_ge(sems["smisc"], const_ld)
                        lm1 = inc(v.tensor_tensor(smalls[:1, 0:128],
                                                  lrow[:1, 0:128],
                                                  lrow[:1, 128:256],
                                                  op=ALU.mult), "sv")
                        v.wait_ge(sems["sv"], lm1)
                        rg = inc(v.reduce_sum(smalls[:1, 130:131],
                                              smalls[:1, 0:128],
                                              axis=AX, opt_output=False), "sv")
                        v.wait_ge(sems["sv"], rg)
                        lm2 = inc(v.tensor_tensor(smalls[:1, 0:128],
                                                  lrow[:1, 256:384],
                                                  lrow[:1, 384:512],
                                                  op=ALU.mult), "sv")
                        v.wait_ge(sems["sv"], lm2)
                        inc(v.reduce_sum(smalls[:1, 131:132], smalls[:1, 0:128],
                                         axis=AX, opt_output=False), "sv")
                    lsum_done = val("sv")

                    @blk0.scalar
                    def _(s):
                        s.wait_ge(sems["sv"], lsum_done)
                        inc(s.activation(smalls[:1, 133:135], smalls[:1, 130:132],
                                         AF.Exp), "se")
                    lexp_done = val("se")

                    @blk0.vector
                    def _(v):
                        v.wait_ge(sems["se"], lexp_done)
                        inc(v.scalar_tensor_tensor(smalls[:1, 136:137],
                                                   smalls[:1, 134:135],
                                                   -LAMBDA_INIT,
                                                   smalls[:1, 133:134],
                                                   op0=ALU.add,
                                                   op1=ALU.subtract), "sv")
                    neglam_done = val("sv")

                    @blk0.sync
                    def _(sync):
                        sync.wait_ge(sems["sv"], neglam_done)
                        d1 = inc(sync.dma_start(
                            out=bass.AP(bscr, 40, [[1, 1]]),
                            in_=smalls[:1, 136:137]), "smisc", 16)
                        sync.wait_ge(sems["smisc"], d1)
                        inc(sync.dma_start(
                            out=lam_bc[:, :],
                            in_=bass.AP(bscr, 40, [[0, 128], [1, 1]])),
                            "smisc", 16)
                        inc(sync.dma_start(
                            out=g_bc[:, :],
                            in_=bass.AP(gvec, 0, [[0, 128], [1, 256]])),
                            "smisc", 16)
                    lamg_dma = val("smisc")

                    @blk0.vector
                    def _(v):
                        v.wait_ge(sems["smisc"], lamg_dma)
                        inc(v.tensor_scalar_mul(g_bc[:, :], g_bc[:, :],
                                                1.0 - LAMBDA_INIT), "sv")

            with ExitStack() as p2:
                e2 = p2.enter_context
                kT = [e2(nc.sbuf_tensor(f"kT{i}", [128, T], BF16))
                      for i in range(2)]
                vh = [e2(nc.sbuf_tensor(f"vh{i}", [128, 64 * 258], BF16))
                      for i in range(2)]
                aT = [e2(nc.sbuf_tensor(f"aT{i}", [128, 1024], BF16))
                      for i in range(2)]
                t0b = e2(nc.sbuf_tensor("t0b", [128, 8 * 256], F32))
                dump2 = e2(nc.sbuf_tensor("dump2", [128, 256], BF16))
                Oacc = [e2(nc.psum_tensor(f"Oacc{i}", [128, 512], F32))
                        for i in range(4)]
                Sps = [e2(nc.psum_tensor(f"Sps{i}", [128, 1024], F32))
                       for i in range(2)]

                blk1 = e2(nc.Block())

                @blk1.sync
                def _(sync):
                    sync.wait_ge(sems["scc"], AGK_DONE)
                    inc(sync.dma_start(
                        out=kmx_all[:1, :],
                        in_=bass.AP(agk_out, KSZ,
                                    [[128, 1], [1, 16], [KSZ2, 8]]),
                        allow_slow_non_contiguous=True), "smisc", 16)
                kmx_ld = val("smisc")

                @blk1.vector
                def _(v):
                    v.wait_ge(sems["smisc"], kmx_ld)
                    last = None
                    for hc in range(16):
                        last = v.reduce_max(kg[:1, hc:hc + 1],
                                     kmx_all[:1, hc * 8:(hc + 1) * 8], axis=AX, opt_output=False)
                    rg = inc(last, "sv")
                    v.wait_ge(sems["sv"], rg)
                    v.wait_ge(sems["sv"], qn_done_ref[0])
                    v.scalar_tensor_tensor(smalls[:1, 0:16], qm2[:1, 0:16],
                                           1.0609, kg[:1, :],
                                           op0=ALU.mult, op1=ALU.mult)
                    inc(v.scalar_tensor_tensor(smalls[:1, 16:32],
                                               qm2[:1, 16:32], 1.0609,
                                               kg[:1, :],
                                               op0=ALU.mult, op1=ALU.mult),
                        "sv")
                m2_done = val("sv")

                @blk1.scalar
                def _(s):
                    s.wait_ge(sems["sv"], m2_done)
                    ln1 = inc(s.activation(smalls[:1, 32:64],
                                           smalls[:1, 0:32], AF.Ln), "se")
                    s.wait_ge(sems["se"], ln1)
                    inc(s.activation(smalls[:1, 32:64], smalls[:1, 32:64],
                                     AF.Exp, bias=0.0, scale=0.5),
                        "se")
                mexp_done = val("se")

                @blk1.vector
                def _(v):
                    v.wait_ge(sems["se"], mexp_done)
                    inc(v.tensor_scalar_mul(smalls[:1, 32:64],
                                            smalls[:1, 32:64], -1.0), "sv")
                negm_row = val("sv")

                @blk1.sync
                def _(sync):
                    sync.wait_ge(sems["sv"], negm_row)
                    d1 = inc(sync.dma_start(
                        out=bass.AP(bscr, 0, [[1, 32]]),
                        in_=smalls[:1, 32:64]), "smisc", 16)
                    sync.wait_ge(sems["smisc"], d1)
                    inc(sync.dma_start(
                        out=negM[:, :],
                        in_=bass.AP(bscr, 0, [[0, 128], [1, 32]])),
                        "smisc", 16)
                negm_dma = val("smisc")

                @blk1.vector
                def _(v):
                    v.wait_ge(sems["smisc"], negm_dma)
                    v.memset(Sps[0][:, :], 0.0)
                    inc(v.memset(Sps[1][:, :], 0.0), "sv")
                p2init = val("sv")

                ST_S = {}
                ST_AV = {}
                SE_E = {}
                gpair = 0
                sweep_evac_done = p2init
                hc_last_av = {}
                kv_k_done = {}
                kv_v_done = [0] * 8

                for h in range(8):
                    @blk1.sync
                    def _(sync, h=h):
                        buf = h % 2
                        for comp in range(2):
                            hc = 2 * h + comp
                            if hc >= 2 and (hc - 2) in hc_last_av:
                                sync.wait_ge(sems["st"], hc_last_av[hc - 2])
                            kv_k_done[hc] = inc(sync.dma_start(
                                out=kT[hc % 2][:, :],
                                in_=bass.AP(agk_out, KOFF + hc * 128 * 1024,
                                            [[1024, 128], [KSZ2, 8],
                                             [1, 1024]])),
                                f"sk{hc % 2}", 16)
                        if h == 0:
                            sync.wait_ge(sems["scc"], AGV_DONE)
                        if h >= 2:
                            sync.wait_ge(sems["st"], hc_last_av[2 * h - 3])
                        for b in range(NC):
                            kv_v_done[h] = inc(sync.dma_start(
                                out=bass.AP(vh[buf], b * 8 * 258,
                                            [[64 * 258, 128], [258, 8],
                                             [1, 258]]),
                                in_=bass.AP(agv_out, b * VSZ + h * 258,
                                            [[VROW, 128], [128 * VROW, 8],
                                             [1, 258]])), f"svh{h % 2}", 16)

                    for comp in range(2):
                        hc = 2 * h + comp
                        for grp in range(2):
                            maxj = 32 if grp == 0 else 64
                            pairs = []
                            j = 0
                            while j < maxj:
                                pairs.append((j, j + 1 if j + 1 < maxj else None))
                                j += 2
                            P = len(pairs)
                            base_g = gpair
                            negm_col = grp * 16 + hc

                            sv_base = val("sv")
                            mask_cum = {}
                            cum = 0
                            for pi, pr in enumerate(pairs):
                                for j in pr:
                                    if j is None:
                                        continue
                                    r = j // 8 - 4 * grp
                                    if 0 <= r <= 3:
                                        cum += 1
                                mask_cum[pi] = sv_base + cum

                            se_base = val("se")
                            for pi in range(P):
                                SE_E[base_g + pi] = se_base + pi + 1

                            @blk1.tensor
                            def _(t, h=h, grp=grp, comp=comp, hc=hc,
                                  pairs=pairs, base_g=base_g, P=P,
                                  sv_base=sv_base, mask_cum=mask_cum,
                                  gate_sv=sweep_evac_done):
                                def emit_S(pi):
                                    g = base_g + pi
                                    reg = g % 2
                                    if g - 2 >= 0:
                                        t.wait_ge(sems["se"], SE_E[g - 2])
                                    if pi == 0:
                                        t.wait_ge(sems[f"sk{hc % 2}"],
                                                  kv_k_done[hc])
                                        t.wait_ge(sems["sv"], p2init)
                                    last = None
                                    for idx, j in enumerate(pairs[pi]):
                                        if j is None:
                                            continue
                                        r0 = band_r0(grp, j)
                                        last = t.matmul(
                                            Sps[reg][:, idx * 512 + 128 * r0:
                                                     (idx + 1) * 512],
                                            kT[hc % 2][:, j * 128:(j + 1) * 128],
                                            qT[:, hc * 1024 + grp * 512 + 128 * r0:
                                               hc * 1024 + (grp + 1) * 512],
                                            start=True, stop=True)
                                    ST_S[g] = inc(last, "st")

                                def emit_AV(pi):
                                    g = base_g + pi
                                    reg = g % 2
                                    t.wait_ge(sems["se"], SE_E[g])
                                    prev = mask_cum[pi - 1] if pi > 0 else sv_base
                                    if mask_cum[pi] > prev:
                                        t.wait_ge(sems["sv"], mask_cum[pi])
                                    if pi == 0:
                                        t.wait_ge(sems["sv"], gate_sv)
                                        t.wait_ge(sems[f"svh{h % 2}"],
                                                  kv_v_done[h])
                                    last = None
                                    for idx, j in enumerate(pairs[pi]):
                                        if j is None:
                                            continue
                                        r0 = band_r0(grp, j)
                                        for r in range(r0, 4):
                                            last = t.matmul(
                                                Oacc[r][:, 0:257],
                                                aT[reg][:, idx * 512 + r * 128:
                                                        idx * 512 + (r + 1) * 128],
                                                vh[h % 2][:, j * 258:j * 258 + 257],
                                                start=(j == 0),
                                                stop=(j == slot_last_j(grp, r)))
                                    ST_AV[g] = inc(last, "st")

                                emit_S(0)
                                for pi in range(1, P):
                                    emit_S(pi)
                                    emit_AV(pi - 1)
                                emit_AV(P - 1)

                            @blk1.scalar
                            def _(s, grp=grp, pairs=pairs, base_g=base_g,
                                  P=P, negm_col=negm_col):
                                for pi in range(P):
                                    g = base_g + pi
                                    j0, j1 = pairs[pi]
                                    reg = g % 2
                                    s.wait_ge(sems["st"], ST_S[g])
                                    if g - 2 >= 0:
                                        s.wait_ge(sems["st"], ST_AV[g - 2])
                                    r0p = band_r0(grp, j0)
                                    w = (1024 if j1 is not None else 512) \
                                        - 128 * r0p
                                    got = inc(s.activation(
                                        aT[reg][:, 128 * r0p:128 * r0p + w],
                                        Sps[reg][:, 128 * r0p:128 * r0p + w],
                                        AF.Exp,
                                        bias=negM[:, negm_col:negm_col + 1],
                                        scale=1.0), "se")
                                    assert got == SE_E[g]

                            @blk1.vector
                            def _(v, grp=grp, pairs=pairs, base_g=base_g,
                                  mask_cum=mask_cum):
                                for pi, pr in enumerate(pairs):
                                    g = base_g + pi
                                    reg = g % 2
                                    for idx, j in enumerate(pr):
                                        if j is None:
                                            continue
                                        r = j // 8 - 4 * grp
                                        if r < 0 or r > 3:
                                            continue
                                        v.wait_ge(sems["se"], SE_E[g])
                                        inc(v.tensor_tensor(
                                            aT[reg][:, idx * 512 + r * 128:
                                                    idx * 512 + (r + 1) * 128],
                                            aT[reg][:, idx * 512 + r * 128:
                                                    idx * 512 + (r + 1) * 128],
                                            masks[:, (j % 8) * 128:
                                                  (j % 8 + 1) * 128],
                                            op=ALU.mult), "sv")
                                assert val("sv") == mask_cum[len(pairs) - 1]

                            gpair += P
                            last_g = gpair - 1

                            if comp == 0:
                                @blk1.vector
                                def _(v, grp=grp, last_g=last_g):
                                    v.wait_ge(sems["st"], ST_AV[last_g])
                                    last = None
                                    for r in range(4):
                                        last = v.reciprocal(fin_sc[:, r:r + 1],
                                                            Oacc[r][:, 256:257])
                                    rg = inc(last, "sv")
                                    v.wait_ge(sems["sv"], rg)
                                    for r in range(4):
                                        last = v.tensor_scalar_mul(
                                            t0b[:, (grp * 4 + r) * 256:
                                                (grp * 4 + r + 1) * 256],
                                            Oacc[r][:, 0:256],
                                            fin_sc[:, r:r + 1])
                                    inc(last, "sv")
                                sweep_evac_done = val("sv")
                            else:
                                fin_v1 = {}

                                @blk1.vector
                                def _(v, grp=grp, last_g=last_g,
                                      fin_v1=fin_v1):
                                    v.wait_ge(sems["st"], ST_AV[last_g])
                                    last = None
                                    for r in range(4):
                                        last = v.reciprocal(fin_sc[:, r:r + 1],
                                                            Oacc[r][:, 256:257])
                                    rg = inc(last, "sv")
                                    v.wait_ge(sems["sv"], rg)
                                    last = None
                                    for r in range(4):
                                        last = v.tensor_tensor(
                                            fin_sc[:, r:r + 1],
                                            fin_sc[:, r:r + 1],
                                            lam_bc[:, :], op=ALU.mult)
                                    lm = inc(last, "sv")
                                    v.wait_ge(sems["sv"], lm)
                                    for r in range(4):
                                        sl = (grp * 4 + r) * 256
                                        fin_v1[r] = inc(
                                            v.scalar_tensor_tensor(
                                                t0b[:, sl:sl + 256],
                                                Oacc[r][:, 0:256],
                                                fin_sc[:, r:r + 1],
                                                t0b[:, sl:sl + 256],
                                                op0=ALU.mult, op1=ALU.add),
                                            "sv")

                                se_sq = {}

                                @blk1.scalar
                                def _(s, grp=grp, fin_v1=fin_v1,
                                      se_sq=se_sq):
                                    for r in range(4):
                                        sl = (grp * 4 + r) * 256
                                        s.wait_ge(sems["sv"], fin_v1[r])
                                        se_sq[r] = inc(s.activation(
                                            dump2[:, 0:256],
                                            t0b[:, sl:sl + 256],
                                            AF.Square,
                                            accum_out=fin_sc[:, 4 + r:5 + r]),
                                            "se")

                                sv_eps = {}

                                @blk1.vector
                                def _(v, se_sq=se_sq, sv_eps=sv_eps):
                                    for r in range(4):
                                        v.wait_ge(sems["se"], se_sq[r])
                                        sv_eps[r] = inc(v.tensor_scalar(
                                            fin_sc[:, 4 + r:5 + r],
                                            fin_sc[:, 4 + r:5 + r],
                                            1.0 / 256.0, EPS,
                                            op0=ALU.mult, op1=ALU.add), "sv")

                                se_fin = {}

                                @blk1.scalar
                                def _(s, sv_eps=sv_eps, se_fin=se_fin):
                                    for r in range(4):
                                        s.wait_ge(sems["sv"], sv_eps[r])
                                        lw = inc(s.activation(
                                            fin_sc[:, 4 + r:5 + r],
                                            fin_sc[:, 4 + r:5 + r],
                                            AF.Ln, bias=0.0, scale=1.0), "se")
                                        s.wait_ge(sems["se"], lw)
                                        se_fin[r] = inc(s.activation(
                                            fin_sc[:, 4 + r:5 + r],
                                            fin_sc[:, 4 + r:5 + r],
                                            AF.Exp, bias=0.0, scale=-0.5),
                                            "se")

                                @blk1.vector
                                def _(v, h=h, grp=grp, se_fin=se_fin):
                                    for r in range(4):
                                        sl = (grp * 4 + r) * 256
                                        v.wait_ge(sems["se"], se_fin[r])
                                        lt = grp * 4 + r
                                        inc(v.scalar_tensor_tensor(
                                            o_sb[:, lt * E + h * 256:
                                                 lt * E + (h + 1) * 256],
                                            t0b[:, sl:sl + 256],
                                            fin_sc[:, 4 + r:5 + r],
                                            g_bc[:, :],
                                            op0=ALU.mult, op1=ALU.mult), "sv")
                                sweep_evac_done = val("sv")
                        hc_last_av[hc] = ST_AV[gpair - 1]

            p12.close()

            with ExitStack() as p3:
                e3 = p3.enter_context
                oT = e3(nc.sbuf_tensor("oT", [128, NCH * TLOC], BF16))
                x1 = e3(nc.sbuf_tensor("x1", [128, NT_LOC * E], F32))
                bias_bc = e3(nc.sbuf_tensor("bias_bc", [128, E], F32))
                wsl = [e3(nc.sbuf_tensor(f"wsl{i}", [128, 16 * 512], BF16))
                       for i in range(2)]
                h2b = e3(nc.sbuf_tensor("h2b", [128, E], BF16))
                ost = [e3(nc.sbuf_tensor(f"ost{i}", [128, 512], F32))
                       for i in range(2)]

                blk2 = e3(nc.Block())

                @blk2.sync
                def _(sync):
                    for lt in range(NT_LOC):
                        inc(sync.dma_start(
                            out=x1[:, lt * E:(lt + 1) * E],
                            in_=bass.AP(x_q, lt * 128 * E, [[E, 128], [1, E]])),
                            "smisc", 16)
                    inc(sync.dma_start(
                        out=bias_bc[:, :],
                        in_=bass.AP(fbias, 0, [[0, 128], [1, E]])),
                        "smisc", 16)
                x1_ld = val("smisc")

                with ExitStack() as tps:
                    tpps3 = [tps.enter_context(
                        nc.psum_tensor(f"tpb{i}", [128, 512], F32))
                        for i in range(8)]
                    ev_hist = {}
                    for k in range(16):
                        lt, hchunk = k // 2, k % 2
                        quad = tpps3[0:4] if k % 2 == 0 else tpps3[4:8]

                        @blk2.tensor
                        def _(t, lt=lt, hchunk=hchunk, k=k, quad=quad):
                            if k >= 2:
                                t.wait_ge(sems["sv"], ev_hist[k - 2])
                            last = None
                            for c8 in range(8):
                                ch = hchunk * 8 + c8
                                last = t.matmul(
                                    quad[c8 % 4][:, (c8 // 4) * 128:
                                                 (c8 // 4 + 1) * 128],
                                    o_sb[:, lt * E + ch * 128:
                                         lt * E + (ch + 1) * 128],
                                    id16sb[:, :], start=True, stop=True)
                            inc(last, "st")
                        tpd = val("st")

                        @blk2.vector
                        def _(v, lt=lt, hchunk=hchunk, tpd=tpd, k=k, quad=quad):
                            v.wait_ge(sems["st"], tpd)
                            ins = None
                            for par in range(4):
                                ins = v.tensor_copy(
                                    bass.AP(oT,
                                            (hchunk * 8 + par) * 1024 + lt * 128,
                                            [[NCH * TLOC, 128], [4096, 2],
                                             [1, 128]]),
                                    quad[par][:, 0:256])
                            ev_hist[k] = inc(ins, "sv")
                    oT_done = val("sv")

                ost_done = {}

                def emit_mm_phase(accs, lhsT_buf, wsrc, out_to_x1,
                                  lhs_ready_sv, w_gate_scc, w_prev_use,
                                  w_gate_se=0):
                    wdone = {}
                    wuse = {}
                    ds_evac = {}

                    for ds in range(4):
                        @blk2.gpsimd
                        def _(g, ds=ds):
                            if ds == 0 and w_gate_scc:
                                g.wait_ge(sems["scc"], w_gate_scc)
                            if ds == 0 and w_gate_se:
                                g.wait_ge(sems["se"], w_gate_se)
                            if ds < 2 and w_prev_use:
                                g.wait_ge(sems["st"], w_prev_use[ds])
                            if ds >= 2:
                                g.wait_ge(sems["st"], wuse[ds - 2])
                            for din in range(16):
                                inc(g.dma_start(
                                    out=wsl[ds % 2][:, din * 512:(din + 1) * 512],
                                    in_=bass.AP(wsrc, din * 128 * E + ds * 512,
                                                [[E, 128], [1, 512]])),
                                    f"sw{ds % 2}", 16)
                        wdone[ds] = (f"sw{ds % 2}", val(f"sw{ds % 2}"))

                        @blk2.tensor
                        def _(t, ds=ds):
                            t.wait_ge(sems["sv"], lhs_ready_sv)
                            if ds >= 1:
                                t.wait_ge(sems["sv"], ds_evac[ds - 1])
                            t.wait_ge(sems[wdone[ds][0]], wdone[ds][1])
                            for din in range(16):
                                for lt in range(NT_LOC):
                                    acc = accs[lt // 4]
                                    i = lt % 4
                                    last = t.matmul(
                                        acc[:, i * 512:(i + 1) * 512],
                                        lhsT_buf[:, din * 1024 + lt * 128:
                                                 din * 1024 + (lt + 1) * 128],
                                        wsl[ds % 2][:, din * 512:(din + 1) * 512],
                                        start=(din == 0), stop=(din == 15))
                            wuse[ds] = inc(last, "st")

                        @blk2.vector
                        def _(v, ds=ds):
                            v.wait_ge(sems["st"], wuse[ds])
                            v.wait_ge(sems["smisc"], x1_ld)
                            last = None
                            for lt in range(NT_LOC):
                                acc = accs[lt // 4]
                                i = lt % 4
                                xsl = x1[:, lt * E + ds * 512:
                                         lt * E + (ds + 1) * 512]
                                if out_to_x1:
                                    last = v.scalar_tensor_tensor(
                                        xsl, acc[:, i * 512:(i + 1) * 512],
                                        1.0, xsl, op0=ALU.mult, op1=ALU.add)
                                else:
                                    idx = ds * NT_LOC + lt
                                    if idx >= 2:
                                        v.wait_ge(sems[f"so{idx % 2}"],
                                                  16 * (idx // 2))
                                    ost_done[(ds, lt)] = inc(
                                        v.scalar_tensor_tensor(
                                            ost[idx % 2][:, :],
                                            acc[:, i * 512:(i + 1) * 512],
                                            1.0, xsl,
                                            op0=ALU.mult, op1=ALU.add), "sv")
                            if out_to_x1:
                                ds_evac[ds] = inc(last, "sv")
                            else:
                                ds_evac[ds] = val("sv")

                        if not out_to_x1:
                            @blk2.sync
                            def _(sync, ds=ds):
                                for lt in range(NT_LOC):
                                    idx = ds * NT_LOC + lt
                                    sync.wait_ge(sems["sv"],
                                                 ost_done[(ds, lt)])
                                    got = inc(sync.dma_start(
                                        out=bass.AP(out,
                                                    lt * 128 * E + ds * 512,
                                                    [[E, 128], [1, 512]]),
                                        in_=ost[idx % 2][:, :]),
                                        f"so{idx % 2}", 16)
                                    assert got == 16 * (idx // 2 + 1)
                    return val("sv"), wuse

                with ExitStack() as accsc:
                    pa = accsc.enter_context(
                        nc.psum_tensor("pa3", [128, 2048], F32))
                    pb = accsc.enter_context(
                        nc.psum_tensor("pb3", [128, 2048], F32))
                    wo_done, wo_use = emit_mm_phase(
                        (pa, pb), oT, Wo, True, oT_done, 0, None)

                ln2_last = 0
                ln2_sq_last = [0]
                for lt in range(NT_LOC):
                    @blk2.vector
                    def _(v, lt=lt):
                        if lt == 0:
                            v.wait_ge(sems["sv"], wo_done)
                            v.wait_ge(sems["smisc"], x1_ld)
                        xsl = x1[:, lt * E:(lt + 1) * E]
                        rg = inc(v.reduce_sum(st_m[:, :], xsl, axis=AX,
                                              opt_output=False), "sv")
                        v.wait_ge(sems["sv"], rg)
                        inc(v.tensor_scalar_mul(st_m[:, :], st_m[:, :],
                                                1.0 / E), "sv")
                    msum = val("sv")

                    @blk2.scalar
                    def _(s, lt=lt, msum=msum):
                        s.wait_ge(sems["sv"], msum)
                        inc(s.activation(wsl[0][:, 0:E],
                                         x1[:, lt * E:(lt + 1) * E],
                                         AF.Square, accum_out=st_ss[:, :]),
                            "se")
                    ssum = val("se")
                    ln2_sq_last[0] = ssum

                    @blk2.vector
                    def _(v, lt=lt, ssum=ssum):
                        v.wait_ge(sems["se"], ssum)
                        v.tensor_scalar(st_v1[:, :], st_ss[:, :], 1.0 / E,
                                        EPS, op0=ALU.mult, op1=ALU.add)
                        vw = inc(v.tensor_tensor(st_v2[:, :], st_m[:, :],
                                                 st_m[:, :], op=ALU.mult),
                                 "sv")
                        v.wait_ge(sems["sv"], vw)
                        inc(v.tensor_tensor(st_v1[:, :], st_v1[:, :],
                                            st_v2[:, :], op=ALU.subtract),
                            "sv")
                    vdone = val("sv")

                    @blk2.scalar
                    def _(s, lt=lt, vdone=vdone):
                        s.wait_ge(sems["sv"], vdone)
                        lw = inc(s.activation(st_r[:, :], st_v1[:, :], AF.Ln,
                                              bias=0.0, scale=8.0), "se")
                        s.wait_ge(sems["se"], lw)
                        inc(s.activation(st_r[:, :], st_r[:, :], AF.Exp,
                                         bias=0.0, scale=-0.5), "se")
                    rdone = val("se")

                    @blk2.vector
                    def _(v, lt=lt, rdone=rdone):
                        v.wait_ge(sems["se"], rdone)
                        xsl = x1[:, lt * E:(lt + 1) * E]
                        v.tensor_scalar(h2b[:, :], xsl, st_m[:, :],
                                        st_r[:, :], op0=ALU.subtract,
                                        op1=ALU.mult)
                        inc(v.tensor_tensor(xsl, xsl, bias_bc[:, :],
                                            op=ALU.add), "sv")
                    h2_done = val("sv")

                    with ExitStack() as tp2:
                        tpps4 = [tp2.enter_context(
                            nc.psum_tensor(f"tpc{lt}_{i}", [128, 512], F32))
                            for i in range(8)]
                        for hchunk in range(2):
                            quad = tpps4[0:4] if hchunk == 0 else tpps4[4:8]

                            @blk2.tensor
                            def _(t, lt=lt, hchunk=hchunk, h2_done=h2_done,
                                  quad=quad):
                                t.wait_ge(sems["sv"], h2_done)
                                last = None
                                for c8 in range(8):
                                    ch = hchunk * 8 + c8
                                    last = t.matmul(
                                        quad[c8 % 4][:, (c8 // 4) * 128:
                                                     (c8 // 4 + 1) * 128],
                                        h2b[:, ch * 128:(ch + 1) * 128],
                                        id16sb[:, :], start=True, stop=True)
                                inc(last, "st")
                            tpd = val("st")

                            @blk2.vector
                            def _(v, lt=lt, hchunk=hchunk, tpd=tpd, quad=quad):
                                v.wait_ge(sems["st"], tpd)
                                ins = None
                                for par in range(4):
                                    ins = v.tensor_copy(
                                        bass.AP(oT,
                                                (hchunk * 8 + par) * 1024 + lt * 128,
                                                [[NCH * TLOC, 128], [4096, 2],
                                                 [1, 128]]),
                                        quad[par][:, 0:256])
                                inc(ins, "sv")
                    ln2_last = val("sv")

                with ExitStack() as accsc2:
                    pa = accsc2.enter_context(
                        nc.psum_tensor("pa4", [128, 2048], F32))
                    pb = accsc2.enter_context(
                        nc.psum_tensor("pb4", [128, 2048], F32))
                    emit_mm_phase((pa, pb), oT, ar_out, False, ln2_last,
                                  AR_DONE, [wo_use[2], wo_use[3]],
                                  w_gate_se=ln2_sq_last[0])

                @blk2.sync
                def _(sync):
                    sync.wait_ge(sems["so0"], val("so0"))
                    sync.wait_ge(sems["so1"], val("so1"))

    return nc


_CACHE = {}


def _get_program():
    if "nc" not in _CACHE:
        _CACHE["nc"] = build_program()
    return _CACHE["nc"]


def _np32(a):
    return np.asarray(a).astype(np.float32, copy=False)


def kernel(L0_attn_x, L0_attn_Wq, L0_attn_Wk, L0_attn_Wv, L0_attn_Wo,
           L0_attn_lambda_q1, L0_attn_lambda_k1, L0_attn_lambda_q2,
           L0_attn_lambda_k2, L0_attn_g, L0_ffn_factors, L0_ffn_bias):
    x = np.ascontiguousarray(_np32(L0_attn_x)[0])
    Wq_ = np.ascontiguousarray(_np32(L0_attn_Wq))
    Wk_ = np.ascontiguousarray(_np32(L0_attn_Wk))
    Wv_ = np.ascontiguousarray(_np32(L0_attn_Wv))
    Wo_ = np.ascontiguousarray(_np32(L0_attn_Wo))
    fct = np.ascontiguousarray(_np32(L0_ffn_factors))
    fb = np.ascontiguousarray(_np32(L0_ffn_bias))[None, :]
    lam = np.ascontiguousarray(np.stack([
        _np32(L0_attn_lambda_q1), _np32(L0_attn_lambda_k1),
        _np32(L0_attn_lambda_q2), _np32(L0_attn_lambda_k2)]))
    g = np.ascontiguousarray(_np32(L0_attn_g))[None, :]
    id16m = np.eye(128, dtype=ml_dtypes.bfloat16)
    tri = np.triu(np.ones((128, 128), dtype=np.float32))

    nc = _get_program()
    in_maps = []
    for r in range(NC):
        qtiles = [8 * s + r for s in range(8)]
        x_q = np.concatenate([x[t * 128:(t + 1) * 128] for t in qtiles], axis=0)
        x_kv = x[r * 1024:(r + 1) * 1024]
        bm = np.zeros((8, 128, 128), dtype=np.float32)
        for jp in range(8):
            if jp < r:
                bm[jp] = 1.0
            elif jp == r:
                bm[jp] = tri
        in_maps.append({
            "x_q": np.ascontiguousarray(x_q),
            "x_kv": np.ascontiguousarray(x_kv),
            "Wq": Wq_, "Wk": Wk_, "Wv": Wv_, "Wo": Wo_,
            "fac": np.ascontiguousarray(fct[r]),
            "fbias": fb, "lamv": lam, "gvec": g,
            "id16": id16m,
            "bmask": np.ascontiguousarray(
                bm.reshape(8 * 128, 128).astype(ml_dtypes.bfloat16)),
        })

    res = run_bass_kernel_spmd(nc, in_maps, core_ids=list(range(NC)))

    outp = np.zeros((T, E), dtype=np.float32)
    for r in range(NC):
        o = np.asarray(res.results[r]["out"])
        for s in range(8):
            t = 8 * s + r
            outp[t * 128:(t + 1) * 128] = o[s * 128:(s + 1) * 128]
    return outp[None, :, :]
